# version 39
# baseline (speedup 1.0000x reference)
import contextlib
import numpy as np

import concourse.bass as bass
import concourse.mybir as mybir
from concourse.bass_utils import run_bass_kernel_spmd

F32 = mybir.dt.float32
BF16 = mybir.dt.bfloat16
I32 = mybir.dt.int32
AF = mybir.ActivationFunctionType
ALU = mybir.AluOpType
AX = mybir.AxisListType

T = 2048
C = 1024
H3 = 3072
S127 = float(np.float32(1.0 / 127.0))
S127SQ = float(np.float32(1.0 / 127.0)) ** 2

WC = 0
CKA = 8192
CKV2 = 10240
VA = 12288
QTB = 16448
ETB = 20544
YT1 = 22592
WPC = 26688
ACOLS = 34880
ABC = 12288
BBC = 14336
WQB = 16448
WQKB = 20544
HP = 0
WFB = 24576
APR = 28672
WDB = 28672

WFI = 0
WDI = 0


class KB:
    def __init__(self):
        self.nc = bass.Bass("TRN2", target_bir_lowering=False, debug=False)
        self.cnt = {"T": 0, "V": 0, "S": 0}
        for i in range(8):
            self.cnt["Q%d" % i] = 0
        for i in range(4):
            self.cnt["G%d" % i] = 0
        self.q = {"tensor": [], "vector": [], "scalar": [], "sync": [],
                  "gpsimd": []}
        self.pr = {}

    def flush(self):
        for eng in list(self.q):
            lst = self.q[eng]
            if lst:
                def body(h, lst=lst):
                    for f in lst:
                        f(h)
                getattr(self.block, eng)(body)
            self.q[eng] = []

    def _w(self, e, waits):
        for k, v in waits.items():
            if v and v > 0:
                e.wait_ge(self.sem[k],
                          v * 16 if k[0] in ("Q", "G") else v)

    def _mk(self, eng, waits, fn, inc_key=None):
        w = dict(waits)
        sem = self.sem[inc_key] if inc_key else None
        amt = 16 if (inc_key and inc_key[0] in ("Q", "G")) else 1
        def g(e, w=w, fn=fn, sem=sem, amt=amt):
            self._w(e, w)
            r = fn(e)
            if sem is not None:
                r.then_inc(sem, amt)
        self.q[eng].append(g)
        if inc_key:
            self.cnt[inc_key] += 1
            return self.cnt[inc_key]
        return None

    def t(self, waits, fn, inc=False):
        r = self._mk("tensor", waits, fn, "T" if inc else None)
        return r if r is not None else self.cnt["T"]

    def v(self, waits, fn):
        return self._mk("vector", waits, fn, "V")

    def s(self, waits, fn):
        return self._mk("scalar", waits, fn, "S")

    def dq(self, waits, fn, q="Q7"):
        return self._mk("sync", waits, fn, q)

    def dg(self, waits, fn, q="G0"):
        return self._mk("gpsimd", waits, fn, q)

    def pread(self, keys, eng, val):
        for key in keys:
            d = self.pr.setdefault(key, {})
            d[eng] = max(d.get(eng, 0), val)

    def pwait(self, keys):
        out = {}
        for key in keys:
            for eng, v in self.pr.get(key, {}).items():
                out[eng] = max(out.get(eng, 0), v)
        return out

    @staticmethod
    def merge(*ds):
        out = {}
        for d in ds:
            for k, v in d.items():
                out[k] = max(out.get(k, 0), v)
        return out

    def bc_src(self, tens, cols):
        return bass.AP(tens, 0, [[1, 1], [0, 128], [1, cols]])

    def build(self):
        nc = self.nc
        self.d_x = nc.dram_tensor("xT", [2 * C, T], F32, kind="ExternalInput").ap()
        self.d_wq = nc.dram_tensor("wq", [512, C], F32, kind="ExternalInput").ap()
        self.d_wqkT = nc.dram_tensor("wqkT", [512, C], F32, kind="ExternalInput").ap()
        self.d_bq = nc.dram_tensor("bq", [512, 1], F32, kind="ExternalInput").ap()
        self.d_bqk = nc.dram_tensor("bqk", [C, 1], F32, kind="ExternalInput").ap()
        self.d_wkvT = nc.dram_tensor("wkvT", [C, 64], F32, kind="ExternalInput").ap()
        self.d_bkv = nc.dram_tensor("bkv", [64, 1], F32, kind="ExternalInput").ap()
        self.d_wvu = nc.dram_tensor("wvu", [65, 1040], F32, kind="ExternalInput").ap()
        self.d_wpT = nc.dram_tensor("wpT", [C, C], F32, kind="ExternalInput").ap()
        self.d_bp = nc.dram_tensor("bp", [C, 1], F32, kind="ExternalInput").ap()
        self.d_wg = nc.dram_tensor("wgT", [C, H3], I32, kind="ExternalInput").ap()
        self.d_wu = nc.dram_tensor("wuT", [C, H3], I32, kind="ExternalInput").ap()
        self.d_wd = nc.dram_tensor("wdT", [H3, C], I32, kind="ExternalInput").ap()
        self.d_out = nc.dram_tensor("outT", [2 * C, T], F32,
                                    kind="ExternalOutput").ap()
        self.d_wdb = nc.dram_tensor("wdB", [H3, C], BF16, kind="Internal").ap()

        with contextlib.ExitStack() as es:
            sb = lambda n, s, d: es.enter_context(nc.sbuf_tensor(n, s, d))
            self.AR = sb("AR", [128, ACOLS], BF16)
            self.IT = sb("IT", [128, 2048], I32)
            self.xn = sb("xn", [128, 16384], BF16)
            self.sT = sb("sTb", [128, 16384], BF16)
            self.wkv_bf = sb("wkv_bf", [128, 512], BF16)
            self.wvu_bf = sb("wvu_bf", [65, 1040], BF16)
            self.xs = sb("xs", [128, 2048], F32)
            self.CONVST = sb("CONVST", [128, 1024], BF16)
            self.scr = sb("scr", [128, 1024], F32)
            self.outst = sb("outst", [128, 1536], F32)
            self.gsc = sb("gsc", [128, 1024], F32)
            self.R1 = sb("R1", [1, 2048], F32)
            self.R2 = sb("R2", [1, 2048], F32)
            self.R3 = sb("R3", [1, 2048], F32)
            self.R4 = sb("R4", [1, 2048], F32)
            self.arow_bf = sb("arow_bf", [1, 2048], BF16)
            self.brow_bf = sb("brow_bf", [1, 2048], BF16)
            self.bqc = sb("bqc", [128, 4], F32)
            self.bqc_bf = sb("bqc_bf", [128, 4], BF16)
            self.bqkc = sb("bqkc", [128, 8], F32)
            self.bkv_sb = sb("bkv_sb", [64, 1], F32)
            self.bpc = sb("bpc", [128, 8], F32)
            self.bcomb = sb("bcomb", [128, 8], F32)
            self.ones_f = sb("ones_f", [128, 1], F32)
            self.onesr = sb("onesr", [1, 64], F32)
            self.qnrow = sb("qnrow", [1, 16], F32)
            self.mrow = sb("mrow", [1, 16], F32)
            self.cn = [sb("cn0", [1, 1], F32), sb("cn1", [1, 1], F32)]
            self.Cb = sb("Cb", [128, 2], F32)
            self.P = {}
            for nm in ("P01", "P23", "P45", "P67"):
                self.P[nm] = es.enter_context(nc.psum_tensor(nm, [128, 1024], F32))
            self.sem = {}
            for k in (["T", "V", "S"] + ["Q%d" % i for i in range(8)]
                      + ["G%d" % i for i in range(4)]):
                self.sem[k] = es.enter_context(nc.semaphore("SEM" + k))
            self.block = es.enter_context(nc.Block())

            self.init_phase()
            self.flush()
            self.entT, self.entV, self.entS = self.cnt["T"], self.cnt["V"], 0
            self.entG1, self.entG2 = self.cnt["G1"], self.cnt["G2"]
            self.entG0 = self.cnt["G0"]
            for b in range(2):
                self.fold(b)
                self.flush()
                self.ln(b)
                self.flush()
                self.qkv(b)
                self.flush()
                self.st = {"exp_m1": self.entS}
                for blk in range(4):
                    self.attn_blk(b, blk)
                    self.flush()
                self.ffn(b)
                self.flush()
                self.entT, self.entV, self.entS = (self.cnt["T"], self.cnt["V"],
                                                   self.cnt["S"])
                self.entG0 = self.cnt["G0"]
                self.entG1, self.entG2 = self.cnt["G1"], self.cnt["G2"]
            tot = {k: self.cnt[k] for k in self.cnt if k[0] in ("Q", "G")}
            def fin(e, tot=tot):
                for k, v in tot.items():
                    if v:
                        e.wait_ge(self.sem[k], 16 * v)
            self.q["gpsimd"].append(fin)
            self.flush()
        return nc

    def init_phase(self):
        d = self
        for k in range(4):
            d.dq({}, lambda e, k=k: e.dma_start(
                out=d.bqc[:, k:k + 1], in_=d.d_bq[128 * k:128 * k + 128, :]))
        for k in range(8):
            d.dq({}, lambda e, k=k: e.dma_start(
                out=d.bqkc[:, k:k + 1], in_=d.d_bqk[128 * k:128 * k + 128, :]))
        d.dq({}, lambda e: e.dma_start(out=d.bkv_sb[:, :], in_=d.d_bkv))
        for k in range(8):
            d.dq({}, lambda e, k=k: e.dma_start(
                out=d.bpc[:, k:k + 1], in_=d.d_bp[128 * k:128 * k + 128, :]))
        for k in range(8):
            d.dq({}, lambda e, k=k: e.dma_start(
                out=d.scr[:, 64 * k:64 * k + 64],
                in_=d.d_wkvT[128 * k:128 * k + 128, :]))
        q1 = d.dq({}, lambda e: e.dma_start(out=d.xs[0:65, 0:1040], in_=d.d_wvu))
        d.v({"Q7": q1}, lambda e: e.tensor_copy(d.wkv_bf[:, :], d.scr[:, 0:512]))
        d.v({}, lambda e: e.tensor_copy(d.bqc_bf[:, :], d.bqc[:, :]))
        d.v({}, lambda e: e.tensor_copy(d.wvu_bf[:, :], d.xs[0:65, 0:1040]))
        d.v({}, lambda e: e.memset(d.ones_f[:, :], 1.0))
        d.v({}, lambda e: e.memset(d.onesr[:, :], 1.0))
        self.cvp_c = {}
        self.stp_c = {}
        self._bcg_prev = 0
        self.wdb_done = {}
        for pc in range(48):
            self.conv_piece(pc)
        self.v_init = d.cnt["V"]
        self.entV = self.v_init

    def conv_piece(self, pc):
        d = self
        r0 = 128 * (pc // 2)
        c0 = 512 * (pc % 2)
        par = pc % 4
        sl = pc % 2
        qn_ = "Q5" if sl == 0 else "Q7"
        gq = "G1" if sl == 0 else "G2"
        lq = d.dq({"V": d.cvp_c.get(pc - 4, 0)},
                  lambda e, r0=r0, c0=c0, par=par:
                  e.dma_start(out=d.IT[:, 512 * par:512 * par + 512],
                              in_=d.d_wd[r0:r0 + 128, c0:c0 + 512]), q=qn_)
        wc = {qn_: lq, gq: d.stp_c.get(pc - 2, 0)}
        d.cvp_c[pc] = d.v(wc, lambda e, par=par, sl=sl: e.tensor_copy(
            d.CONVST[:, 512 * sl:512 * sl + 512],
            d.IT[:, 512 * par:512 * par + 512]))
        d.stp_c[pc] = d.dg({"V": d.cvp_c[pc]},
                           lambda e, r0=r0, c0=c0, sl=sl:
                           e.dma_start(out=d.d_wdb[r0:r0 + 128, c0:c0 + 512],
                                       in_=d.CONVST[:, 512 * sl:512 * sl + 512]),
                           q=gq)
        if pc == 47:
            self.wdb_done = {"G1": d.stp_c[46], "G2": d.stp_c[47]}

    def fold(self, b):
        d = self
        AR = d.AR
        entT, entV, entS = self.entT, self.entV, self.entS
        cv = []
        for n in range(8):
            src = d.d_wq if n < 4 else d.d_wqkT
            k = n % 4
            dst_off = (WQB if n < 4 else WQKB) + 1024 * k
            stage = d.scr[:, :] if n % 2 == 0 else d.xs[:, 0:1024]
            qn_ = "Q4" if n % 2 == 0 else "Q2"
            wv = {"V": cv[n - 2] if n >= 2 else entV}
            qq = d.dq(wv, lambda e, src=src, k=k, stage=stage: e.dma_start(
                out=stage, in_=src[128 * k:128 * k + 128, :]), q=qn_)
            wv2 = {qn_: qq}
            if n == 0:
                wv2["T"] = entT
                wv2["S"] = entS
            cv.append(d.v(wv2, lambda e, dst_off=dst_off, stage=stage:
                          e.tensor_copy(AR[:, dst_off:dst_off + 1024],
                                        stage)))
        conv_done = cv[-1]
        wq_done = cv[3]

        slots = [("P01", 0), ("P01", 512), ("P23", 0), ("P23", 512)]
        wc_ev = []
        n = 0
        for i in range(8):
            for j2 in range(2):
                pn, off = slots[n % 4]
                pt = d.P[pn]
                for ql in range(4):
                    w = {}
                    if ql == 0:
                        w["V"] = wc_ev[n - 4] if n >= 4 else conv_done
                        w = d.merge(w, d.pwait([(pn, off // 512)]))
                    d.t(w, lambda e, pt=pt, off=off, ql=ql, i=i, j2=j2: e.matmul(
                        pt[:, off:off + 512],
                        AR[:, WQB + 1024 * ql + 128 * i:
                           WQB + 1024 * ql + 128 * i + 128],
                        AR[:, WQKB + 1024 * ql + 512 * j2:
                           WQKB + 1024 * ql + 512 * j2 + 512],
                        start=(ql == 0), stop=(ql == 3)), inc=(ql == 3))
                tt = d.cnt["T"]
                wev = {"T": tt}
                ev = d.v(wev, lambda e, pt=pt, off=off, i=i, j2=j2:
                         e.tensor_copy(AR[:, WC + 1024 * i + 512 * j2:
                                          WC + 1024 * i + 512 * j2 + 512],
                                       pt[:, off:off + 512]))
                d.pread([(pn, off // 512)], "V", ev)
                wc_ev.append(ev)
                n += 1
        for j in range(8):
            off = 512 * (j % 2)
            for ql in range(4):
                w = {}
                if ql == 0:
                    w = d.pwait([("P67", off // 512)])
                    if j == 0:
                        w = d.merge(w, {"V": wq_done})
                d.t(w, lambda e, j=j, ql=ql, off=off: e.matmul(
                    d.P["P67"][:, off:off + 1],
                    AR[:, WQKB + 1024 * ql + 128 * j:
                       WQKB + 1024 * ql + 128 * j + 128],
                    d.bqc_bf[:, ql:ql + 1], start=(ql == 0), stop=(ql == 3)),
                    inc=(ql == 3))
            tt = d.cnt["T"]
            ev = d.v({"T": tt}, lambda e, j=j, off=off: e.tensor_tensor(
                d.bcomb[:, j:j + 1], d.P["P67"][:, off:off + 1],
                d.bqkc[:, j:j + 1], ALU.add))
            d.pread([("P67", off // 512)], "V", ev)
        self.fold_T = d.cnt["T"]
        self.fold_V = d.cnt["V"]

    def ln(self, b):
        d = self
        AR = d.AR
        xrow = 1024 * b
        prevT = 0
        prevV = self.fold_V
        for half in range(2):
            mmT = [0] * 8
            sq_v = [0] * 8
            for k in range(8):
                slot = 1024 * (k % 2)
                wv = ({"T": mmT[k - 2], "V": sq_v[k - 2]} if k >= 2 else
                      {"T": prevT, "V": prevV})
                qn_ = "Q0" if k % 2 == 0 else "Q1"
                ld = d.dq(wv, lambda e, k=k, half=half, slot=slot: e.dma_start(
                    out=d.xs[:, slot:slot + 1024],
                    in_=d.d_x[xrow + 128 * k:xrow + 128 * k + 128,
                              1024 * half:1024 * half + 1024]), q=qn_)
                sqs = d.gsc if k % 2 == 0 else d.outst
                wsq = {qn_: ld}
                if k >= 2:
                    wsq["T"] = mmT[k - 2]
                if half == 0 and k <= 1:
                    wsq["G1"] = self.entG1
                    wsq["G2"] = self.entG2
                sq_v[k] = d.v(wsq, lambda e, slot=slot, sqs=sqs: e.tensor_tensor(
                    sqs[:, 0:1024], d.xs[:, slot:slot + 1024],
                    d.xs[:, slot:slot + 1024], ALU.mult))
                w0 = {qn_: ld}
                if k == 0:
                    w0 = d.merge(w0, d.pwait([("P45", 0), ("P45", 1)]))
                for nb in range(2):
                    d.t(w0 if nb == 0 else {}, lambda e, slot=slot, nb=nb, k=k:
                        e.matmul(d.P["P45"][0:1, 512 * nb:512 * nb + 512],
                                 d.ones_f[:, :],
                                 d.xs[:, slot + 512 * nb:slot + 512 * nb + 512],
                                 start=(k == 0), stop=(k == 7)), inc=False)
                wsq2 = {"V": sq_v[k]}
                if k == 0:
                    wsq2 = d.merge(wsq2, d.pwait([("P67", 0), ("P67", 1)]))
                for nb in range(2):
                    d.t(wsq2 if nb == 0 else {}, lambda e, sqs=sqs, nb=nb, k=k:
                        e.matmul(d.P["P67"][0:1, 512 * nb:512 * nb + 512],
                                 d.ones_f[:, :],
                                 sqs[:, 512 * nb:512 * nb + 512],
                                 start=(k == 0), stop=(k == 7)),
                        inc=(nb == 1))
                mmT[k] = d.cnt["T"]
            tt = d.cnt["T"]
            for nb in range(2):
                ev = d.v({"T": tt}, lambda e, half=half, nb=nb: e.tensor_copy(
                    d.R1[0:1, 1024 * half + 512 * nb:
                         1024 * half + 512 * nb + 512],
                    d.P["P45"][0:1, 512 * nb:512 * nb + 512]))
                d.pread([("P45", nb)], "V", ev)
                ev = d.v({}, lambda e, half=half, nb=nb: e.tensor_copy(
                    d.R2[0:1, 1024 * half + 512 * nb:
                         1024 * half + 512 * nb + 512],
                    d.P["P67"][0:1, 512 * nb:512 * nb + 512]))
                d.pread([("P67", nb)], "V", ev)
                prevV = ev
            prevT = tt
        d.v({"T": self.entT, "G0": self.entG0}, lambda e: e.tensor_scalar_mul(
            d.R3[:, :], d.R1[:, :], 1.0 / 1024.0))
        d.v({}, lambda e: e.tensor_scalar_mul(d.R4[:, :], d.R2[:, :],
                                              1.0 / 1024.0))
        d.v({}, lambda e: e.tensor_tensor(d.R1[:, :], d.R3[:, :],
                                          d.R3[:, :], ALU.mult))
        d.v({}, lambda e: e.tensor_tensor(d.R2[:, :], d.R4[:, :],
                                          d.R1[:, :], ALU.subtract))
        vv = d.v({}, lambda e: e.tensor_scalar_add(d.R4[:, :], d.R2[:, :],
                                                   1e-5))
        ss = d.s({"V": vv}, lambda e: e.activation(d.R2[:, :], d.R4[:, :],
                                                   AF.Sqrt))
        d.v({"S": ss}, lambda e: e.reciprocal(d.R4[:, :], d.R2[:, :]))
        d.v({}, lambda e: e.scalar_tensor_tensor(
            d.R1[:, :], d.R3[:, :], -1.0, d.R4[:, :], ALU.mult, ALU.mult))
        d.v({}, lambda e: e.tensor_copy(d.arow_bf[:, :], d.R4[:, :]))
        vv = d.v({}, lambda e: e.tensor_copy(d.brow_bf[:, :], d.R1[:, :]))
        g1 = d.dg({"V": vv, "T": self.entT, "S": self.entS},
                  lambda e: e.dma_start(out=AR[:, ABC:ABC + 2048],
                                        in_=d.bc_src(d.arow_bf, 2048)))
        g2 = d.dg({}, lambda e: e.dma_start(out=AR[:, BBC:BBC + 2048],
                                            in_=d.bc_src(d.brow_bf, 2048)))
        xv = [0] * 16
        for half in range(2):
            for k in range(8):
                i = half * 8 + k
                slot = 1024 * (i % 2)
                wv = {"V": xv[i - 2]} if i >= 2 else {"V": vv}
                qn_ = "Q0" if i % 2 == 0 else "Q1"
                ld = d.dq(wv, lambda e, k=k, half=half, slot=slot: e.dma_start(
                    out=d.xs[:, slot:slot + 1024],
                    in_=d.d_x[xrow + 128 * k:xrow + 128 * k + 128,
                              1024 * half:1024 * half + 1024]), q=qn_)
                tmp = d.gsc if i % 2 == 0 else d.outst
                w1 = {qn_: ld, "G0": g2}
                if i == 0:
                    w1["T"] = self.entT
                d.v(w1, lambda e, slot=slot, tmp=tmp, half=half: e.tensor_tensor(
                    tmp[:, 0:1024], d.xs[:, slot:slot + 1024],
                    AR[:, ABC + 1024 * half:ABC + 1024 * half + 1024], ALU.mult))
                xv[i] = d.v({}, lambda e, tmp=tmp, half=half, k=k:
                            e.tensor_tensor(
                    d.xn[:, 2048 * k + 1024 * half:
                         2048 * k + 1024 * half + 1024],
                    tmp[:, 0:1024],
                    AR[:, BBC + 1024 * half:BBC + 1024 * half + 1024], ALU.add))
        self.ln_V = d.cnt["V"]

    def qkv(self, b):
        d = self
        AR = d.AR
        m1 = d.v({"T": self.entT, "S": self.entS},
                 lambda e: e.memset(AR[64:65, CKA:CKA + 2048], 1.0))
        ck_ev = [0] * 4
        for nb in range(4):
            off = 512 * (nb % 2)
            for c in range(8):
                w = {}
                if c == 0:
                    w = d.pwait([("P45", nb % 2)])
                    w = d.merge(w, {"V": self.ln_V if nb < 2 else ck_ev[nb - 2]})
                d.t(w, lambda e, off=off, c=c, nb=nb: e.matmul(
                    d.P["P45"][0:64, off:off + 512],
                    d.wkv_bf[:, 64 * c:64 * c + 64],
                    d.xn[:, 2048 * c + 512 * nb:2048 * c + 512 * nb + 512],
                    start=(c == 0), stop=(c == 7)), inc=(c == 7))
            tt = d.cnt["T"]
            ev1 = d.v({"T": tt, "V": m1}, lambda e, off=off, nb=nb:
                      e.tensor_scalar(
                          AR[0:64, CKA + 512 * nb:CKA + 512 * nb + 512],
                          d.P["P45"][0:64, off:off + 512], d.bkv_sb[:, :], None,
                          ALU.add))
            d.pread([("P45", nb % 2)], "V", ev1)
            d.v({}, lambda e, nb=nb: e.tensor_copy(
                AR[0:64, CKV2 + 512 * nb:CKV2 + 512 * nb + 512],
                AR[0:64, CKA + 512 * nb:CKA + 512 * nb + 512]))
            ck_ev[nb] = d.v({}, lambda e, nb=nb: e.tensor_copy(
                AR[64:128, CKV2 + 512 * nb:CKV2 + 512 * nb + 512],
                AR[0:64, CKA + 512 * nb:CKA + 512 * nb + 512]))
        k2mmT = [0] * 4
        for nb in range(4):
            off = 512 * (nb % 2)
            wsq = {}
            if nb >= 1:
                wsq["T"] = k2mmT[nb - 1]
            sqv = d.v(wsq, lambda e, nb=nb: e.tensor_tensor(
                d.gsc[0:64, 0:512],
                AR[0:64, CKV2 + 512 * nb:CKV2 + 512 * nb + 512],
                AR[0:64, CKV2 + 512 * nb:CKV2 + 512 * nb + 512], ALU.mult))
            w = d.merge({"V": sqv}, d.pwait([("P67", nb % 2)]))
            k2mmT[nb] = d.t(w, lambda e, off=off: e.matmul(
                d.P["P67"][0:1, off:off + 512], d.ones_f[0:64, :],
                d.gsc[0:64, 0:512], start=True, stop=True), inc=True)
            ev = d.v({"T": k2mmT[nb]}, lambda e, off=off, nb=nb: e.tensor_copy(
                d.R4[0:1, 512 * nb:512 * nb + 512],
                d.P["P67"][0:1, off:off + 512]))
            d.pread([("P67", nb % 2)], "V", ev)
        d.v({}, lambda e: e.tensor_reduce(d.mrow[0:1, 0:1], d.R4[:, :], AX.X,
                                          ALU.max))
        self.kmax_V = d.cnt["V"]
        wp_cv = []
        for k in range(8):
            stage = d.scr[:, :] if k % 2 == 0 else d.xs[:, 0:1024]
            qn_ = "Q4" if k % 2 == 0 else "Q2"
            wv = {"V": wp_cv[k - 2]} if k >= 2 else {"V": self.ln_V}
            qq = d.dq(wv, lambda e, k=k, stage=stage: e.dma_start(
                out=stage, in_=d.d_wpT[128 * k:128 * k + 128, :]), q=qn_)
            w2 = {qn_: qq}
            if k == 0:
                w2["T"] = self.entT
                w2["S"] = self.entS
            wp_cv.append(d.v(w2, lambda e, k=k, stage=stage: e.tensor_copy(
                AR[:, WPC + 1024 * k:WPC + 1024 * k + 1024], stage)))
        self.wp_V = d.cnt["V"]
        self.ck_V = ck_ev[3]

    def attn_blk(self, b, blk):
        d = self
        AR = d.AR
        st = self.st
        tq0 = 512 * blk
        cbslot = blk % 2
        if b == 0 and False:
            for pc in range(12 * blk, 12 * blk + 12):
                self.conv_piece(pc)
        qev = [0] * 8
        qn_red = [0] * 8
        qn_mmT = [0] * 8
        for k in range(8):
            off = 512 * (k % 2)
            if k < 2:
                w = d.merge(d.pwait([("P01", k % 2)]), {"V": self.wp_V})
            else:
                w = {"V": qev[k - 2]}
            for c in range(8):
                d.t(w if c == 0 else {}, lambda e, off=off, c=c, k=k, tq0=tq0:
                    e.matmul(
                    d.P["P01"][:, off:off + 512],
                    AR[:, WC + 1024 * c + 128 * k:WC + 1024 * c + 128 * k + 128],
                    d.xn[:, 2048 * c + tq0:2048 * c + tq0 + 512],
                    start=(c == 0), stop=(c == 7)), inc=(c == 7))
            tt = d.cnt["T"]
            qev[k] = d.v({"T": tt}, lambda e, off=off, k=k: e.tensor_scalar(
                AR[:, QTB + 512 * k:QTB + 512 * k + 512],
                d.P["P01"][:, off:off + 512], d.bcomb[:, k:k + 1], None,
                ALU.add))
            d.pread([("P01", k % 2)], "V", qev[k])
            gslot = 512 * (k % 2)
            wv = {"T": qn_mmT[k - 2]} if k >= 2 else {}
            sqv = d.v(wv, lambda e, k=k, gslot=gslot: e.tensor_tensor(
                d.gsc[:, gslot:gslot + 512],
                AR[:, QTB + 512 * k:QTB + 512 * k + 512],
                AR[:, QTB + 512 * k:QTB + 512 * k + 512], ALU.mult))
            w2 = d.merge({"V": sqv}, d.pwait([("P23", k % 2)]))
            qn_mmT[k] = d.t(w2, lambda e, gslot=gslot, off=off: e.matmul(
                d.P["P23"][0:1, off:off + 512], d.ones_f[:, :],
                d.gsc[:, gslot:gslot + 512], start=True, stop=True), inc=True)
            qn_red[k] = d.v({"T": qn_mmT[k]}, lambda e, k=k, off=off:
                            e.tensor_reduce(d.qnrow[0:1, k:k + 1],
                                            d.P["P23"][0:1, off:off + 512],
                                            AX.X, ALU.max))
            d.pread([("P23", k % 2)], "V", qn_red[k])
        d.v({"V": self.kmax_V}, lambda e: e.tensor_reduce(
            d.mrow[0:1, 1:2], d.qnrow[0:1, 0:8], AX.X, ALU.max))
        d.v({}, lambda e: e.tensor_tensor(d.mrow[0:1, 2:3], d.mrow[0:1, 1:2],
                                          d.mrow[0:1, 0:1], ALU.add))
        cvv = d.v({"G3": st.get("gcb_m2", 0)}, lambda e, cbslot=cbslot:
                  e.tensor_single_scalar(d.cn[cbslot][:, :], d.mrow[0:1, 2:3],
                                         -1.0 / 16.0, ALU.mult))
        wcb = {"V": cvv, "S": st.get("exp_m2", self.entS)}
        gcb = d.dg(wcb, lambda e, cbslot=cbslot: e.dma_start(
            out=d.Cb[:, cbslot:cbslot + 1],
            in_=bass.AP(d.cn[cbslot], 0, [[1, 1], [0, 128], [1, 1]])), q="G3")

        def vgen(p, extra_w):
            vaoff = VA + 2080 * (p % 2)
            pb = "P45" if p % 2 == 0 else "P67"
            last = 0
            for ttile in range(16):
                off = 512 * (ttile % 2)
                w = d.pwait([(pb, ttile % 2)])
                if ttile == 0:
                    w = d.merge(w, {"V": self.ck_V}, extra_w)
                mt = d.t(w, lambda e, p=p, ttile=ttile, off=off, pb=pb:
                         e.matmul(
                    d.P[pb][:, off:off + 130],
                    AR[0:65, CKA + 128 * ttile:CKA + 128 * ttile + 128],
                    d.wvu_bf[:, 130 * p:130 * p + 130],
                    start=True, stop=True), inc=True)
                last = d.v({"T": mt}, lambda e, vaoff=vaoff, ttile=ttile,
                           off=off, pb=pb: e.tensor_copy(
                    AR[:, vaoff + 130 * ttile:vaoff + 130 * ttile + 130],
                    d.P[pb][:, off:off + 130]))
                d.pread([(pb, ttile % 2)], "V", last)
            return last

        vg_done = {0: vgen(0, {"T": st.get("attnv_6", 0)})}
        exp_S = {}
        attnv_T = {}

        def emit_scores(g):
            p, j = g // 16, g % 16
            pn = "P01" if g % 2 == 0 else "P23"
            w = d.pwait([(pn, 0), (pn, 1)])
            if g < 4:
                w = d.merge(w, {"V": max(qn_red[7], self.ck_V)})
            if j < 2:
                w = d.merge(w, {"V": qev[p]})
            d.t(w, lambda e, pn=pn, j=j, p=p: e.matmul(
                d.P[pn][:, 0:512],
                AR[0:64, CKV2 + 128 * j:CKV2 + 128 * j + 128],
                AR[0:64, QTB + 512 * p:QTB + 512 * p + 512],
                start=True, stop=True, tile_position=(0, 0)), inc=False)
            smT = d.t({}, lambda e, pn=pn, j=j, p=p: e.matmul(
                d.P[pn][:, 512:1024],
                AR[64:128, CKV2 + 128 * j:CKV2 + 128 * j + 128],
                AR[64:128, QTB + 512 * p:QTB + 512 * p + 512],
                start=True, stop=True, tile_position=(64, 0)), inc=True)
            we = {"T": max(smT * 0, smT), "G3": gcb}
            we["T"] = max(smT, attnv_T.get(g - 2, st.get("attnv_7", 0)))
            ss = d.s(we, lambda e, pn=pn, g=g, cbslot=cbslot: e.activation(
                AR[:, ETB + 1024 * (g % 2):ETB + 1024 * (g % 2) + 1024],
                d.P[pn][:, 0:1024], AF.Exp,
                bias=d.Cb[:, cbslot:cbslot + 1], scale=0.125))
            exp_S[g] = ss
            d.pread([(pn, 0), (pn, 1)], "S", ss)

        def emit_attnv(g):
            p, ja = g // 16, g % 16
            pb = "P45" if p % 2 == 0 else "P67"
            wa = {"S": exp_S[g]}
            if ja == 0:
                wa = d.merge(wa, {"V": vg_done[p]},
                             d.pwait([(pb, 0), (pb, 1)]))
            vaoff = VA + 2080 * (p % 2)
            d.t(wa, lambda e, ja=ja, vaoff=vaoff, pb=pb, g=g: e.matmul(
                d.P[pb][0:65, 0:512],
                AR[:, vaoff + 130 * ja:vaoff + 130 * ja + 65],
                AR[:, ETB + 1024 * (g % 2):ETB + 1024 * (g % 2) + 512],
                start=(ja == 0), stop=(ja == 15)), inc=False)
            at = d.t({}, lambda e, ja=ja, vaoff=vaoff, pb=pb, g=g: e.matmul(
                d.P[pb][0:65, 512:1024],
                AR[:, vaoff + 130 * ja + 65:vaoff + 130 * ja + 130],
                AR[:, ETB + 1024 * (g % 2) + 512:
                   ETB + 1024 * (g % 2) + 1024],
                start=(ja == 0), stop=(ja == 15)), inc=True)
            attnv_T[g] = at
            if ja == 15:
                emit_norm(p)

        def emit_norm(p):
            pb = "P45" if p % 2 == 0 else "P67"
            d.v({"T": attnv_T[16 * p + 15]}, lambda e, pb=pb: e.tensor_copy(
                d.outst[0:65, 0:512], d.P[pb][0:65, 0:512]))
            cpv = d.v({}, lambda e, pb=pb: e.tensor_copy(
                d.outst[0:65, 512:1024], d.P[pb][0:65, 512:1024]))
            d.pread([(pb, 0), (pb, 1)], "V", cpv)
            dg1 = d.dg({"V": cpv, "G0": self._bcg_prev},
                       lambda e: e.dma_start(
                out=d.R3[0:1, 0:1024], in_=d.outst[64:65, 0:1024]))
            bcg = d.dg({"G0": dg1}, lambda e: e.dma_start(
                out=d.gsc[0:64, 0:1024],
                in_=bass.AP(d.R3, 0, [[1, 1], [0, 64], [1, 1024]])))
            self._bcg_prev = bcg
            d.v({"G0": bcg}, lambda e: e.reciprocal(
                d.scr[0:64, 0:1024], d.gsc[0:64, 0:1024]))
            d.v({}, lambda e, p=p: e.tensor_tensor(
                AR[0:64, YT1 + 512 * p:YT1 + 512 * p + 512],
                d.outst[0:64, 0:512], d.scr[0:64, 0:512], ALU.mult))
            y2 = d.v({}, lambda e, p=p: e.tensor_tensor(
                AR[64:128, YT1 + 512 * p:YT1 + 512 * p + 512],
                d.outst[0:64, 512:1024], d.scr[0:64, 512:1024],
                ALU.mult))
            self._ylast = y2

        for g in range(130):
            if g >= 2:
                emit_attnv(g - 2)
            if g < 128:
                emit_scores(g)
            if g % 16 == 13 and g // 16 < 7:
                p = g // 16
                vg_done[p + 1] = vgen(
                    p + 1, {"T": attnv_T.get(16 * (p - 1) + 15,
                                             st.get("attnv_7", 0))})
        ylast = self._ylast
        slots = [("P01", 0), ("P01", 512), ("P23", 0), ("P23", 512)]
        for cc in range(8):
            pn, off = slots[cc % 4]
            pt = d.P[pn]
            for k in range(8):
                w = {}
                if k == 0:
                    w = d.pwait([(pn, off // 512)])
                    if cc == 0:
                        w = d.merge(w, {"V": ylast})
                d.t(w, lambda e, pt=pt, off=off, k=k, cc=cc: e.matmul(
                    pt[:, off:off + 512],
                    AR[:, WPC + 1024 * k + 128 * cc:
                       WPC + 1024 * k + 128 * cc + 128],
                    AR[:, YT1 + 512 * k:YT1 + 512 * k + 512],
                    start=(k == 0), stop=(k == 7)), inc=(k == 7))
            tt = d.cnt["T"]
            ev = d.v({"T": tt}, lambda e, pt=pt, off=off, cc=cc, tq0=tq0:
                     e.tensor_scalar(
                d.sT[:, 2048 * cc + tq0:2048 * cc + tq0 + 512],
                pt[:, off:off + 512], d.bpc[:, cc:cc + 1], None, ALU.add))
            d.pread([(pn, off // 512)], "V", ev)
        st["attnv_6"] = attnv_T[16 * 6 + 15]
        st["attnv_7"] = attnv_T[16 * 7 + 15]
        st["exp_m2"] = st.get("exp_m1", self.entS)
        st["exp_m1"] = exp_S[127]
        st["gcb_m2"] = st.get("gcb_m1", 0)
        st["gcb_m1"] = gcb

    def ffn(self, b):
        d = self
        AR = d.AR
        xrow = 1024 * b
        ffnT, ffnS = self.cnt["T"], self.cnt["S"]
        hp_v = {}
        silu_S = {}
        cv_v = {}
        cvg_v = {}
        gu_byi = {}
        prev_half_T = 0
        for half in range(2):
            for hc in range(24):
                i = half * 24 + hc
                par = i % 2
                ga_pn = "P01" if par == 0 else "P45"
                gb_pn = "P23" if par == 0 else "P67"
                if i == 0:
                    wq_ = {"V": self.entV}
                elif hc == 0:
                    wq_ = {"V": cvg_v[i - 1] if False else cv_v[i - 1]}
                else:
                    wq_ = {"S": cvg_v[i - 1]}
                wq2 = dict(wq_) if i == 0 or hc == 0 else {"S": cv_v[i - 1]}
                qg = d.dq(wq_, lambda e, hc=hc: e.dma_start(
                    out=bass.AP(d.IT, WFI,
                                [[2048, 128], [128, 8], [1, 128]]),
                    in_=bass.AP(d.d_wg.tensor, 128 * hc,
                                [[3072, 128], [393216, 8], [1, 128]])),
                          q="Q5")
                qu = d.dq(wq2, lambda e, hc=hc: e.dma_start(
                    out=bass.AP(d.IT, WFI + 1024,
                                [[2048, 128], [128, 8], [1, 128]]),
                    in_=bass.AP(d.d_wu.tensor, 128 * hc,
                                [[3072, 128], [393216, 8], [1, 128]])),
                          q="Q7")
                wcv = {"Q5": qg, "T": gu_byi.get(i - 2, 0)}
                if i == 0:
                    wcv = d.merge(wcv, {"T": ffnT, "S": ffnS})
                cvg_v[i] = d.s(wcv, lambda e, par=par: e.copy(
                    AR[:, WFB + 2048 * par:WFB + 2048 * par + 1024],
                    d.IT[:, WFI:WFI + 1024]))
                cv_v[i] = d.s({"Q7": qu}, lambda e, par=par: e.copy(
                    AR[:, WFB + 2048 * par + 1024:WFB + 2048 * par + 2048],
                    d.IT[:, WFI + 1024:WFI + 2048]))
                guT = {}
                for m in range(2):
                    pn = ga_pn if m == 0 else gb_pn
                    boff = 1024 * m
                    for tb in range(2):
                        w = {}
                        if m == 0 and tb == 0:
                            w = d.merge({"S": cvg_v[i]},
                                        d.pwait([(ga_pn, 0), (ga_pn, 1),
                                                 (gb_pn, 0), (gb_pn, 1)]))
                        if m == 1 and tb == 0:
                            w = {"S": cv_v[i]}
                        for c in range(8):
                            d.t(w if c == 0 else {}, lambda e, pn=pn, tb=tb,
                                c=c, par=par, boff=boff, half=half: e.matmul(
                                d.P[pn][:, 512 * tb:512 * tb + 512],
                                AR[:, WFB + 2048 * par + boff + 128 * c:
                                   WFB + 2048 * par + boff + 128 * c + 128],
                                d.xn[:, 2048 * c + 1024 * half + 512 * tb:
                                     2048 * c + 1024 * half + 512 * tb + 512],
                                start=(c == 0), stop=(c == 7)),
                                inc=(c == 7 and tb == 1))
                    guT[m] = d.cnt["T"]
                gu_byi[i] = guT[1]
                wsl = {"T": guT[0]}
                if i >= 2:
                    wsl["V"] = hp_v[i - 2]
                ssl = d.s(wsl, lambda e, ga_pn=ga_pn, par=par: e.activation(
                    AR[:, APR + 1024 * par:APR + 1024 * par + 1024],
                    d.P[ga_pn][:, 0:1024], AF.Silu, scale=S127))
                silu_S[i] = ssl
                d.pread([(ga_pn, 0), (ga_pn, 1)], "S", ssl)
                hv = d.v({"S": ssl, "T": guT[1]},
                         lambda e, gb_pn=gb_pn, par=par, hc=hc: e.tensor_tensor(
                    AR[:, HP + 1024 * hc:HP + 1024 * hc + 1024],
                    AR[:, APR + 1024 * par:APR + 1024 * par + 1024],
                    d.P[gb_pn][:, 0:1024], ALU.mult))
                hp_v[i] = hv
                d.pread([(gb_pn, 0), (gb_pn, 1)], "V", hv)
            dslots = [("P01", 0), ("P01", 512), ("P23", 0), ("P23", 512)]
            o2_v = {}
            dma_g = {}
            down_byc = {}
            wd_q = {}
            hlast = hp_v[half * 24 + 23]
            slast = silu_S[half * 24 + 23]
            for cc in range(8):
                par = cc % 2
                wl = {"T": max(down_byc.get(cc - 2, 0),
                               prev_half_T if cc < 2 else 0)}
                if cc < 2:
                    wl = d.merge(wl, {"V": hlast, "S": slast},
                                 self.wdb_done if half == 0 else {})
                wd_q[cc] = d.dq(wl, lambda e, cc=cc, par=par: e.dma_start(
                    out=bass.AP(d.AR, WDB + 3072 * par,
                                [[ACOLS, 128], [128, 24], [1, 128]]),
                    in_=bass.AP(d.d_wdb.tensor, 128 * cc,
                                [[1024, 128], [131072, 24], [1, 128]])),
                               q="Q6" if par == 0 else "Q4")
                self._wd_cv_last = 0
                for tb in range(2):
                    s4 = (2 * cc + tb) % 4
                    pn, off = dslots[s4]
                    pt = d.P[pn]
                    w = d.merge({"Q6" if par == 0 else "Q4": wd_q[cc]},
                                d.pwait([(pn, off // 512)]))
                    for hcc in range(24):
                        d.t(w if hcc == 0 else {}, lambda e, pt=pt, off=off,
                            hcc=hcc, par=par, tb=tb: e.matmul(
                            pt[:, off:off + 512],
                            AR[:, WDB + 3072 * par + 128 * hcc:
                               WDB + 3072 * par + 128 * hcc + 128],
                            AR[:, HP + 1024 * hcc + 512 * tb:
                               HP + 1024 * hcc + 512 * tb + 512],
                            start=(hcc == 0), stop=(hcc == 23)),
                            inc=(hcc == 23))
                    tt = d.cnt["T"]
                    down_byc[cc] = tt
                    s2 = (2 * cc + tb) % 2
                    wgs = {"T": tt, "V": o2_v.get(2 * cc + tb - 2, 0)}
                    gss = d.s(wgs, lambda e, pt=pt, off=off, s2=s2:
                              e.activation(d.gsc[:, 512 * s2:512 * s2 + 512],
                                           pt[:, off:off + 512], AF.Silu,
                                           scale=S127SQ))
                    d.pread([(pn, off // 512)], "S", gss)
                    xslot = (2 * cc + tb) % 4
                    qn_ = "Q%d" % xslot
                    lq = d.dq({"V": o2_v.get(2 * cc + tb - 4, 0)},
                              lambda e, cc=cc, half=half, tb=tb, xslot=xslot:
                              e.dma_start(
                        out=d.xs[:, 512 * xslot:512 * xslot + 512],
                        in_=d.d_x[xrow + 128 * cc:xrow + 128 * cc + 128,
                                  1024 * half + 512 * tb:
                                  1024 * half + 512 * tb + 512]), q=qn_)
                    d.v({"S": gss}, lambda e, s2=s2, cc=cc, half=half, tb=tb:
                        e.tensor_tensor(
                        d.outst[:, 0:512],
                        d.gsc[:, 512 * s2:512 * s2 + 512],
                        d.sT[:, 2048 * cc + 1024 * half + 512 * tb:
                             2048 * cc + 1024 * half + 512 * tb + 512],
                        ALU.mult))
                    gq = "G%d" % (1 + s2)
                    ov = d.v({qn_: lq, gq: dma_g.get(2 * cc + tb - 2, 0)},
                             lambda e, s2=s2, xslot=xslot: e.tensor_tensor(
                        d.outst[:, 512 + 512 * s2:512 + 512 * s2 + 512],
                        d.outst[:, 0:512],
                        d.xs[:, 512 * xslot:512 * xslot + 512], ALU.add))
                    o2_v[2 * cc + tb] = ov
                    dma_g[2 * cc + tb] = d.dg(
                        {"V": ov}, lambda e, cc=cc, half=half, tb=tb, s2=s2:
                        e.dma_start(
                            out=d.d_out[xrow + 128 * cc:xrow + 128 * cc + 128,
                                        1024 * half + 512 * tb:
                                        1024 * half + 512 * tb + 512],
                            in_=d.outst[:, 512 + 512 * s2:
                                        512 + 512 * s2 + 512]), q=gq)
            prev_half_T = d.cnt["T"]


def kernel(**inputs):
    f32 = np.float32
    x = np.asarray(inputs["L0_attn_x"], f32)
    wq = np.ascontiguousarray(np.asarray(inputs["L0_attn_wq_d"], f32))
    bq = np.ascontiguousarray(np.asarray(inputs["L0_attn_bq_d"], f32).reshape(512, 1))
    wqk = np.asarray(inputs["L0_attn_w_qk"], f32)
    bqk = np.ascontiguousarray(np.asarray(inputs["L0_attn_b_qk"], f32).reshape(1024, 1))
    wkv = np.asarray(inputs["L0_attn_wkv_d"], f32)
    bkv = np.ascontiguousarray(np.asarray(inputs["L0_attn_bkv_d"], f32).reshape(64, 1))
    wvu = np.asarray(inputs["L0_attn_wv_u"], f32)
    bvu = np.asarray(inputs["L0_attn_bv_u"], f32)
    wp = np.asarray(inputs["L0_attn_w_proj"], f32)
    bp = np.ascontiguousarray(np.asarray(inputs["L0_attn_b_proj"], f32).reshape(1024, 1))
    wg = np.asarray(inputs["L0_ffn_w_gate"], np.int32)
    wu = np.asarray(inputs["L0_ffn_w_up"], np.int32)
    wd = np.asarray(inputs["L0_ffn_w_down"], np.int32)

    wqkT = np.ascontiguousarray(wqk.T)
    wkvT = np.ascontiguousarray(wkv.T)
    wpT = np.ascontiguousarray(wp.T)
    wgT = np.ascontiguousarray(wg.T)
    wuT = np.ascontiguousarray(wu.T)
    wdT = np.ascontiguousarray(wd.T)
    wvuT = wvu.T
    wvu_aug = np.zeros((65, 1040), f32)
    for h in range(16):
        wvu_aug[0:64, 65 * h:65 * h + 64] = wvuT[:, 64 * h:64 * h + 64]
        wvu_aug[64, 65 * h:65 * h + 64] = bvu[64 * h:64 * h + 64]
        wvu_aug[64, 65 * h + 64] = 1.0
    wvu_aug = np.ascontiguousarray(wvu_aug)

    nc = KB().build()
    in_maps = []
    for c in range(8):
        xsh = np.ascontiguousarray(
            x[2 * c:2 * c + 2].transpose(0, 2, 1)).reshape(2048, 2048)
        in_maps.append({
            "xT": xsh, "wq": wq, "wqkT": wqkT, "bq": bq, "bqk": bqk,
            "wkvT": wkvT, "bkv": bkv, "wvu": wvu_aug, "wpT": wpT, "bp": bp,
            "wgT": wgT, "wuT": wuT, "wdT": wdT,
        })
    res = run_bass_kernel_spmd(nc, in_maps, core_ids=list(range(8)))
    out = np.empty((16, 2048, 1024), f32)
    for c in range(8):
        o = np.asarray(res.results[c]["outT"]).reshape(2, 1024, 2048)
        out[2 * c:2 * c + 2] = o.transpose(0, 2, 1)
    return out


if __name__ == "__main__":
    KB().build()
    print("build ok")


# version 41
# speedup vs baseline: 1.0849x; 1.0849x over previous
import contextlib
import numpy as np

import concourse.bass as bass
import concourse.mybir as mybir
from concourse.bass_utils import run_bass_kernel_spmd

F32 = mybir.dt.float32
BF16 = mybir.dt.bfloat16
I32 = mybir.dt.int32
AF = mybir.ActivationFunctionType
ALU = mybir.AluOpType
AX = mybir.AxisListType

T = 2048
C = 1024
H3 = 3072
S127 = float(np.float32(1.0 / 127.0))
S127SQ = float(np.float32(1.0 / 127.0)) ** 2

WC = 0
CKA = 8192
CKV2 = 10240
VA = 12288
QTB = 16448
ETB = 20544
YT1 = 22592
WPC = 26688
ACOLS = 34880
ABC = 12288
BBC = 14336
WQB = 16448
WQKB = 20544
HP = 0
WFB = 24576
APR = 28672
WDB = 28672

WFI = 0
WDI = 0


class KB:
    def __init__(self):
        self.nc = bass.Bass("TRN2", target_bir_lowering=False, debug=False)
        self.cnt = {"T": 0, "V": 0, "S": 0}
        for i in range(8):
            self.cnt["Q%d" % i] = 0
        for i in range(4):
            self.cnt["G%d" % i] = 0
        self.q = {"tensor": [], "vector": [], "scalar": [], "sync": [],
                  "gpsimd": []}
        self.pr = {}

    def flush(self):
        for eng in list(self.q):
            lst = self.q[eng]
            if lst:
                def body(h, lst=lst):
                    for f in lst:
                        f(h)
                getattr(self.block, eng)(body)
            self.q[eng] = []

    def _w(self, e, waits):
        for k, v in waits.items():
            if v and v > 0:
                e.wait_ge(self.sem[k],
                          v * 16 if k[0] in ("Q", "G") else v)

    def _mk(self, eng, waits, fn, inc_key=None):
        w = dict(waits)
        sem = self.sem[inc_key] if inc_key else None
        amt = 16 if (inc_key and inc_key[0] in ("Q", "G")) else 1
        def g(e, w=w, fn=fn, sem=sem, amt=amt):
            self._w(e, w)
            r = fn(e)
            if sem is not None:
                r.then_inc(sem, amt)
        self.q[eng].append(g)
        if inc_key:
            self.cnt[inc_key] += 1
            return self.cnt[inc_key]
        return None

    def t(self, waits, fn, inc=False):
        r = self._mk("tensor", waits, fn, "T" if inc else None)
        return r if r is not None else self.cnt["T"]

    def v(self, waits, fn):
        return self._mk("vector", waits, fn, "V")

    def s(self, waits, fn):
        return self._mk("scalar", waits, fn, "S")

    def dq(self, waits, fn, q="Q7"):
        return self._mk("sync", waits, fn, q)

    def dg(self, waits, fn, q="G0"):
        return self._mk("gpsimd", waits, fn, q)

    def pread(self, keys, eng, val):
        for key in keys:
            d = self.pr.setdefault(key, {})
            d[eng] = max(d.get(eng, 0), val)

    def pwait(self, keys):
        out = {}
        for key in keys:
            for eng, v in self.pr.get(key, {}).items():
                out[eng] = max(out.get(eng, 0), v)
        return out

    @staticmethod
    def merge(*ds):
        out = {}
        for d in ds:
            for k, v in d.items():
                out[k] = max(out.get(k, 0), v)
        return out

    def bc_src(self, tens, cols):
        return bass.AP(tens, 0, [[1, 1], [0, 128], [1, cols]])

    def build(self):
        nc = self.nc
        self.d_x = nc.dram_tensor("xT", [2 * C, T], F32, kind="ExternalInput").ap()
        self.d_wq = nc.dram_tensor("wq", [512, C], F32, kind="ExternalInput").ap()
        self.d_wqkT = nc.dram_tensor("wqkT", [512, C], F32, kind="ExternalInput").ap()
        self.d_bq = nc.dram_tensor("bq", [512, 1], F32, kind="ExternalInput").ap()
        self.d_bqk = nc.dram_tensor("bqk", [C, 1], F32, kind="ExternalInput").ap()
        self.d_wkvT = nc.dram_tensor("wkvT", [C, 64], F32, kind="ExternalInput").ap()
        self.d_bkv = nc.dram_tensor("bkv", [64, 1], F32, kind="ExternalInput").ap()
        self.d_wvu = nc.dram_tensor("wvu", [65, 1040], F32, kind="ExternalInput").ap()
        self.d_wpT = nc.dram_tensor("wpT", [C, C], F32, kind="ExternalInput").ap()
        self.d_bp = nc.dram_tensor("bp", [C, 1], F32, kind="ExternalInput").ap()
        self.d_wg = nc.dram_tensor("wgT", [C, H3], I32, kind="ExternalInput").ap()
        self.d_wu = nc.dram_tensor("wuT", [C, H3], I32, kind="ExternalInput").ap()
        self.d_wd = nc.dram_tensor("wdT", [H3, C], I32, kind="ExternalInput").ap()
        self.d_out = nc.dram_tensor("outT", [2 * C, T], F32,
                                    kind="ExternalOutput").ap()
        self.d_wdb = nc.dram_tensor("wdB", [H3, C], BF16, kind="Internal").ap()

        with contextlib.ExitStack() as es:
            sb = lambda n, s, d: es.enter_context(nc.sbuf_tensor(n, s, d))
            self.AR = sb("AR", [128, ACOLS], BF16)
            self.IT = sb("IT", [128, 2048], I32)
            self.xn = sb("xn", [128, 16384], BF16)
            self.sT = sb("sTb", [128, 16384], BF16)
            self.wkv_bf = sb("wkv_bf", [128, 512], BF16)
            self.wvu_bf = sb("wvu_bf", [65, 1040], BF16)
            self.xs = sb("xs", [128, 2048], F32)
            self.CONVST = sb("CONVST", [128, 1024], BF16)
            self.scr = sb("scr", [128, 1024], F32)
            self.outst = sb("outst", [128, 1536], F32)
            self.gsc = sb("gsc", [128, 1024], F32)
            self.R1 = sb("R1", [1, 2048], F32)
            self.R2 = sb("R2", [1, 2048], F32)
            self.R3 = sb("R3", [1, 2048], F32)
            self.R4 = sb("R4", [1, 2048], F32)
            self.arow_bf = sb("arow_bf", [1, 2048], BF16)
            self.brow_bf = sb("brow_bf", [1, 2048], BF16)
            self.bqc = sb("bqc", [128, 4], F32)
            self.bqc_bf = sb("bqc_bf", [128, 4], BF16)
            self.bqkc = sb("bqkc", [128, 8], F32)
            self.bkv_sb = sb("bkv_sb", [64, 1], F32)
            self.bpc = sb("bpc", [128, 8], F32)
            self.bcomb = sb("bcomb", [128, 8], F32)
            self.ones_f = sb("ones_f", [128, 1], F32)
            self.onesr = sb("onesr", [1, 64], F32)
            self.qnrow = sb("qnrow", [1, 16], F32)
            self.mrow = sb("mrow", [1, 16], F32)
            self.cn = [sb("cn0", [1, 1], F32), sb("cn1", [1, 1], F32)]
            self.Cb = sb("Cb", [128, 2], F32)
            self.P = {}
            for nm in ("P01", "P23", "P45", "P67"):
                self.P[nm] = es.enter_context(nc.psum_tensor(nm, [128, 1024], F32))
            self.sem = {}
            for k in (["T", "V", "S"] + ["Q%d" % i for i in range(8)]
                      + ["G%d" % i for i in range(4)]):
                self.sem[k] = es.enter_context(nc.semaphore("SEM" + k))
            self.block = es.enter_context(nc.Block())

            self.init_phase()
            self.flush()
            self.entT, self.entV, self.entS = self.cnt["T"], self.cnt["V"], 0
            self.entG1, self.entG2 = self.cnt["G1"], self.cnt["G2"]
            self.entG0 = self.cnt["G0"]
            for b in range(2):
                self.fold(b)
                self.flush()
                self.ln(b)
                self.flush()
                self.qkv(b)
                self.flush()
                self.st = {"exp_m1": self.entS}
                for blk in range(4):
                    self.attn_blk(b, blk)
                    self.flush()
                self.ffn(b)
                self.flush()
                self.entT, self.entV, self.entS = (self.cnt["T"], self.cnt["V"],
                                                   self.cnt["S"])
                self.entG0 = self.cnt["G0"]
                self.entG1, self.entG2 = self.cnt["G1"], self.cnt["G2"]
            tot = {k: self.cnt[k] for k in self.cnt if k[0] in ("Q", "G")}
            def fin(e, tot=tot):
                for k, v in tot.items():
                    if v:
                        e.wait_ge(self.sem[k], 16 * v)
            self.q["gpsimd"].append(fin)
            self.flush()
        return nc

    def init_phase(self):
        d = self
        for k in range(4):
            d.dq({}, lambda e, k=k: e.dma_start(
                out=d.bqc[:, k:k + 1], in_=d.d_bq[128 * k:128 * k + 128, :]))
        for k in range(8):
            d.dq({}, lambda e, k=k: e.dma_start(
                out=d.bqkc[:, k:k + 1], in_=d.d_bqk[128 * k:128 * k + 128, :]))
        d.dq({}, lambda e: e.dma_start(out=d.bkv_sb[:, :], in_=d.d_bkv))
        for k in range(8):
            d.dq({}, lambda e, k=k: e.dma_start(
                out=d.bpc[:, k:k + 1], in_=d.d_bp[128 * k:128 * k + 128, :]))
        for k in range(8):
            d.dq({}, lambda e, k=k: e.dma_start(
                out=d.scr[:, 64 * k:64 * k + 64],
                in_=d.d_wkvT[128 * k:128 * k + 128, :]))
        q1 = d.dq({}, lambda e: e.dma_start(out=d.xs[0:65, 0:1040], in_=d.d_wvu))
        d.v({"Q7": q1}, lambda e: e.tensor_copy(d.wkv_bf[:, :], d.scr[:, 0:512]))
        d.v({}, lambda e: e.tensor_copy(d.bqc_bf[:, :], d.bqc[:, :]))
        d.v({}, lambda e: e.tensor_copy(d.wvu_bf[:, :], d.xs[0:65, 0:1040]))
        d.v({}, lambda e: e.memset(d.ones_f[:, :], 1.0))
        d.v({}, lambda e: e.memset(d.onesr[:, :], 1.0))
        self.cvp_c = {}
        self.stp_c = {}
        self._bcg_prev = 0
        self._pend_proj = None
        self.wdb_done = {}
        self.v_init = d.cnt["V"]
        self.entV = self.v_init

    def conv_piece(self, pc):
        d = self
        r0 = 128 * (pc // 2)
        c0 = 512 * (pc % 2)
        par = pc % 4
        sl = pc % 2
        qn_ = "Q5" if sl == 0 else "Q7"
        gq = "G1" if sl == 0 else "G2"
        lq = d.dq({"V": d.cvp_c.get(pc - 4, 0)},
                  lambda e, r0=r0, c0=c0, par=par:
                  e.dma_start(out=d.IT[:, 512 * par:512 * par + 512],
                              in_=d.d_wd[r0:r0 + 128, c0:c0 + 512]), q=qn_)
        wc = {qn_: lq, gq: d.stp_c.get(pc - 2, 0)}
        d.cvp_c[pc] = d.v(wc, lambda e, par=par, sl=sl: e.tensor_copy(
            d.CONVST[:, 512 * sl:512 * sl + 512],
            d.IT[:, 512 * par:512 * par + 512]))
        d.stp_c[pc] = d.dg({"V": d.cvp_c[pc]},
                           lambda e, r0=r0, c0=c0, sl=sl:
                           e.dma_start(out=d.d_wdb[r0:r0 + 128, c0:c0 + 512],
                                       in_=d.CONVST[:, 512 * sl:512 * sl + 512]),
                           q=gq)
        if pc == 47:
            self.wdb_done = {"G1": d.stp_c[46], "G2": d.stp_c[47]}

    def fold(self, b):
        d = self
        AR = d.AR
        entT, entV, entS = self.entT, self.entV, self.entS
        cv = []
        for n in range(8):
            src = d.d_wq if n < 4 else d.d_wqkT
            k = n % 4
            dst_off = (WQB if n < 4 else WQKB) + 1024 * k
            stage = d.scr[:, :] if n % 2 == 0 else d.xs[:, 0:1024]
            qn_ = "Q4" if n % 2 == 0 else "Q2"
            wv = {"V": cv[n - 2] if n >= 2 else entV}
            qq = d.dq(wv, lambda e, src=src, k=k, stage=stage: e.dma_start(
                out=stage, in_=src[128 * k:128 * k + 128, :]), q=qn_)
            wv2 = {qn_: qq}
            if n == 0:
                wv2["T"] = entT
                wv2["S"] = entS
            cv.append(d.v(wv2, lambda e, dst_off=dst_off, stage=stage:
                          e.tensor_copy(AR[:, dst_off:dst_off + 1024],
                                        stage)))
        conv_done = cv[-1]
        wq_done = cv[3]

        slots = [("P01", 0), ("P01", 512), ("P23", 0), ("P23", 512)]
        wc_ev = []
        n = 0
        for i in range(8):
            for j2 in range(2):
                pn, off = slots[n % 4]
                pt = d.P[pn]
                for ql in range(4):
                    w = {}
                    if ql == 0:
                        w["V"] = wc_ev[n - 4] if n >= 4 else conv_done
                        w = d.merge(w, d.pwait([(pn, off // 512)]))
                    d.t(w, lambda e, pt=pt, off=off, ql=ql, i=i, j2=j2: e.matmul(
                        pt[:, off:off + 512],
                        AR[:, WQB + 1024 * ql + 128 * i:
                           WQB + 1024 * ql + 128 * i + 128],
                        AR[:, WQKB + 1024 * ql + 512 * j2:
                           WQKB + 1024 * ql + 512 * j2 + 512],
                        start=(ql == 0), stop=(ql == 3)), inc=(ql == 3))
                tt = d.cnt["T"]
                wev = {"T": tt}
                ev = d.v(wev, lambda e, pt=pt, off=off, i=i, j2=j2:
                         e.tensor_copy(AR[:, WC + 1024 * i + 512 * j2:
                                          WC + 1024 * i + 512 * j2 + 512],
                                       pt[:, off:off + 512]))
                d.pread([(pn, off // 512)], "V", ev)
                wc_ev.append(ev)
                n += 1
        for j in range(8):
            off = 512 * (j % 2)
            for ql in range(4):
                w = {}
                if ql == 0:
                    w = d.pwait([("P67", off // 512)])
                    if j == 0:
                        w = d.merge(w, {"V": wq_done})
                d.t(w, lambda e, j=j, ql=ql, off=off: e.matmul(
                    d.P["P67"][:, off:off + 1],
                    AR[:, WQKB + 1024 * ql + 128 * j:
                       WQKB + 1024 * ql + 128 * j + 128],
                    d.bqc_bf[:, ql:ql + 1], start=(ql == 0), stop=(ql == 3)),
                    inc=(ql == 3))
            tt = d.cnt["T"]
            ev = d.v({"T": tt}, lambda e, j=j, off=off: e.tensor_tensor(
                d.bcomb[:, j:j + 1], d.P["P67"][:, off:off + 1],
                d.bqkc[:, j:j + 1], ALU.add))
            d.pread([("P67", off // 512)], "V", ev)
        self.fold_T = d.cnt["T"]
        self.fold_V = d.cnt["V"]

    def ln(self, b):
        d = self
        AR = d.AR
        xrow = 1024 * b
        prevT = 0
        prevV = self.fold_V
        for half in range(2):
            mmT = [0] * 8
            sq_v = [0] * 8
            for k in range(8):
                slot = 1024 * (k % 2)
                wv = ({"T": mmT[k - 2], "V": sq_v[k - 2]} if k >= 2 else
                      {"T": prevT, "V": prevV})
                qn_ = "Q0" if k % 2 == 0 else "Q1"
                ld = d.dq(wv, lambda e, k=k, half=half, slot=slot: e.dma_start(
                    out=d.xs[:, slot:slot + 1024],
                    in_=d.d_x[xrow + 128 * k:xrow + 128 * k + 128,
                              1024 * half:1024 * half + 1024]), q=qn_)
                sqs = d.gsc if k % 2 == 0 else d.outst
                wsq = {qn_: ld}
                if k >= 2:
                    wsq["T"] = mmT[k - 2]
                if half == 0 and k <= 1:
                    wsq["G1"] = self.entG1
                    wsq["G2"] = self.entG2
                sq_v[k] = d.v(wsq, lambda e, slot=slot, sqs=sqs: e.tensor_tensor(
                    sqs[:, 0:1024], d.xs[:, slot:slot + 1024],
                    d.xs[:, slot:slot + 1024], ALU.mult))
                w0 = {qn_: ld}
                if k == 0:
                    w0 = d.merge(w0, d.pwait([("P45", 0), ("P45", 1)]))
                for nb in range(2):
                    d.t(w0 if nb == 0 else {}, lambda e, slot=slot, nb=nb, k=k:
                        e.matmul(d.P["P45"][0:1, 512 * nb:512 * nb + 512],
                                 d.ones_f[:, :],
                                 d.xs[:, slot + 512 * nb:slot + 512 * nb + 512],
                                 start=(k == 0), stop=(k == 7)), inc=False)
                wsq2 = {"V": sq_v[k]}
                if k == 0:
                    wsq2 = d.merge(wsq2, d.pwait([("P67", 0), ("P67", 1)]))
                for nb in range(2):
                    d.t(wsq2 if nb == 0 else {}, lambda e, sqs=sqs, nb=nb, k=k:
                        e.matmul(d.P["P67"][0:1, 512 * nb:512 * nb + 512],
                                 d.ones_f[:, :],
                                 sqs[:, 512 * nb:512 * nb + 512],
                                 start=(k == 0), stop=(k == 7)),
                        inc=(nb == 1))
                mmT[k] = d.cnt["T"]
            tt = d.cnt["T"]
            for nb in range(2):
                ev = d.v({"T": tt}, lambda e, half=half, nb=nb: e.tensor_copy(
                    d.R1[0:1, 1024 * half + 512 * nb:
                         1024 * half + 512 * nb + 512],
                    d.P["P45"][0:1, 512 * nb:512 * nb + 512]))
                d.pread([("P45", nb)], "V", ev)
                ev = d.v({}, lambda e, half=half, nb=nb: e.tensor_copy(
                    d.R2[0:1, 1024 * half + 512 * nb:
                         1024 * half + 512 * nb + 512],
                    d.P["P67"][0:1, 512 * nb:512 * nb + 512]))
                d.pread([("P67", nb)], "V", ev)
                prevV = ev
            prevT = tt
        d.v({"T": self.entT, "G0": self.entG0}, lambda e: e.tensor_scalar_mul(
            d.R3[:, :], d.R1[:, :], 1.0 / 1024.0))
        d.v({}, lambda e: e.tensor_scalar_mul(d.R4[:, :], d.R2[:, :],
                                              1.0 / 1024.0))
        d.v({}, lambda e: e.tensor_tensor(d.R1[:, :], d.R3[:, :],
                                          d.R3[:, :], ALU.mult))
        d.v({}, lambda e: e.tensor_tensor(d.R2[:, :], d.R4[:, :],
                                          d.R1[:, :], ALU.subtract))
        vv = d.v({}, lambda e: e.tensor_scalar_add(d.R4[:, :], d.R2[:, :],
                                                   1e-5))
        ss = d.s({"V": vv}, lambda e: e.activation(d.R2[:, :], d.R4[:, :],
                                                   AF.Sqrt))
        d.v({"S": ss}, lambda e: e.reciprocal(d.R4[:, :], d.R2[:, :]))
        d.v({}, lambda e: e.scalar_tensor_tensor(
            d.R1[:, :], d.R3[:, :], -1.0, d.R4[:, :], ALU.mult, ALU.mult))
        d.v({}, lambda e: e.tensor_copy(d.arow_bf[:, :], d.R4[:, :]))
        vv = d.v({}, lambda e: e.tensor_copy(d.brow_bf[:, :], d.R1[:, :]))
        g1 = d.dg({"V": vv, "T": self.entT, "S": self.entS},
                  lambda e: e.dma_start(out=AR[:, ABC:ABC + 2048],
                                        in_=d.bc_src(d.arow_bf, 2048)))
        g2 = d.dg({}, lambda e: e.dma_start(out=AR[:, BBC:BBC + 2048],
                                            in_=d.bc_src(d.brow_bf, 2048)))
        xv = [0] * 16
        for half in range(2):
            for k in range(8):
                i = half * 8 + k
                slot = 1024 * (i % 2)
                wv = {"V": xv[i - 2]} if i >= 2 else {"V": vv}
                qn_ = "Q0" if i % 2 == 0 else "Q1"
                ld = d.dq(wv, lambda e, k=k, half=half, slot=slot: e.dma_start(
                    out=d.xs[:, slot:slot + 1024],
                    in_=d.d_x[xrow + 128 * k:xrow + 128 * k + 128,
                              1024 * half:1024 * half + 1024]), q=qn_)
                tmp = d.gsc if i % 2 == 0 else d.outst
                w1 = {qn_: ld, "G0": g2}
                if i == 0:
                    w1["T"] = self.entT
                d.v(w1, lambda e, slot=slot, tmp=tmp, half=half: e.tensor_tensor(
                    tmp[:, 0:1024], d.xs[:, slot:slot + 1024],
                    AR[:, ABC + 1024 * half:ABC + 1024 * half + 1024], ALU.mult))
                xv[i] = d.v({}, lambda e, tmp=tmp, half=half, k=k:
                            e.tensor_tensor(
                    d.xn[:, 2048 * k + 1024 * half:
                         2048 * k + 1024 * half + 1024],
                    tmp[:, 0:1024],
                    AR[:, BBC + 1024 * half:BBC + 1024 * half + 1024], ALU.add))
        self.ln_V = d.cnt["V"]

    def qkv(self, b):
        d = self
        AR = d.AR
        m1 = d.v({"T": self.entT, "S": self.entS},
                 lambda e: e.memset(AR[64:65, CKA:CKA + 2048], 1.0))
        ck_ev = [0] * 4
        for nb in range(4):
            off = 512 * (nb % 2)
            for c in range(8):
                w = {}
                if c == 0:
                    w = d.pwait([("P45", nb % 2)])
                    w = d.merge(w, {"V": self.ln_V if nb < 2 else ck_ev[nb - 2]})
                d.t(w, lambda e, off=off, c=c, nb=nb: e.matmul(
                    d.P["P45"][0:64, off:off + 512],
                    d.wkv_bf[:, 64 * c:64 * c + 64],
                    d.xn[:, 2048 * c + 512 * nb:2048 * c + 512 * nb + 512],
                    start=(c == 0), stop=(c == 7)), inc=(c == 7))
            tt = d.cnt["T"]
            ev1 = d.v({"T": tt, "V": m1}, lambda e, off=off, nb=nb:
                      e.tensor_scalar(
                          AR[0:64, CKA + 512 * nb:CKA + 512 * nb + 512],
                          d.P["P45"][0:64, off:off + 512], d.bkv_sb[:, :], None,
                          ALU.add))
            d.pread([("P45", nb % 2)], "V", ev1)
            d.v({}, lambda e, nb=nb: e.tensor_copy(
                AR[0:64, CKV2 + 512 * nb:CKV2 + 512 * nb + 512],
                AR[0:64, CKA + 512 * nb:CKA + 512 * nb + 512]))
            ck_ev[nb] = d.v({}, lambda e, nb=nb: e.tensor_copy(
                AR[64:128, CKV2 + 512 * nb:CKV2 + 512 * nb + 512],
                AR[0:64, CKA + 512 * nb:CKA + 512 * nb + 512]))
        k2mmT = [0] * 4
        for nb in range(4):
            off = 512 * (nb % 2)
            wsq = {}
            if nb >= 1:
                wsq["T"] = k2mmT[nb - 1]
            sqv = d.v(wsq, lambda e, nb=nb: e.tensor_tensor(
                d.gsc[0:64, 0:512],
                AR[0:64, CKV2 + 512 * nb:CKV2 + 512 * nb + 512],
                AR[0:64, CKV2 + 512 * nb:CKV2 + 512 * nb + 512], ALU.mult))
            w = d.merge({"V": sqv}, d.pwait([("P67", nb % 2)]))
            k2mmT[nb] = d.t(w, lambda e, off=off: e.matmul(
                d.P["P67"][0:1, off:off + 512], d.ones_f[0:64, :],
                d.gsc[0:64, 0:512], start=True, stop=True), inc=True)
            ev = d.v({"T": k2mmT[nb]}, lambda e, off=off, nb=nb: e.tensor_copy(
                d.R4[0:1, 512 * nb:512 * nb + 512],
                d.P["P67"][0:1, off:off + 512]))
            d.pread([("P67", nb % 2)], "V", ev)
        d.v({}, lambda e: e.tensor_reduce(d.mrow[0:1, 0:1], d.R4[:, :], AX.X,
                                          ALU.max))
        self.kmax_V = d.cnt["V"]
        wp_cv = []
        for k in range(8):
            stage = d.scr[:, :] if k % 2 == 0 else d.xs[:, 0:1024]
            qn_ = "Q4" if k % 2 == 0 else "Q2"
            wv = {"V": wp_cv[k - 2]} if k >= 2 else {"V": self.ln_V}
            qq = d.dq(wv, lambda e, k=k, stage=stage: e.dma_start(
                out=stage, in_=d.d_wpT[128 * k:128 * k + 128, :]), q=qn_)
            w2 = {qn_: qq}
            if k == 0:
                w2["T"] = self.entT
                w2["S"] = self.entS
            wp_cv.append(d.v(w2, lambda e, k=k, stage=stage: e.tensor_copy(
                AR[:, WPC + 1024 * k:WPC + 1024 * k + 1024], stage)))
        self.wp_V = d.cnt["V"]
        self.ck_V = ck_ev[3]

    def attn_blk(self, b, blk):
        d = self
        AR = d.AR
        st = self.st
        tq0 = 512 * blk
        cbslot = blk % 2
        if b == 0:
            for pc in range(12 * blk, 12 * blk + 12):
                self.conv_piece(pc)
        qev = [0] * 8
        qn_red = [0] * 8
        qn_mmT = [0] * 8
        sqv_l = [0] * 8

        def emit_qev(k):
            off = 512 * (k % 2)
            if k < 2:
                w = d.merge(d.pwait([("P01", k % 2)]), {"V": self.wp_V})
            else:
                w = {"V": qev[k - 2]}
            for c in range(8):
                d.t(w if c == 0 else {}, lambda e, off=off, c=c, k=k, tq0=tq0:
                    e.matmul(
                    d.P["P01"][:, off:off + 512],
                    AR[:, WC + 1024 * c + 128 * k:WC + 1024 * c + 128 * k + 128],
                    d.xn[:, 2048 * c + tq0:2048 * c + tq0 + 512],
                    start=(c == 0), stop=(c == 7)), inc=(c == 7))
            tt = d.cnt["T"]
            qev[k] = d.v({"T": tt}, lambda e, off=off, k=k: e.tensor_scalar(
                AR[:, QTB + 512 * k:QTB + 512 * k + 512],
                d.P["P01"][:, off:off + 512], d.bcomb[:, k:k + 1], None,
                ALU.add))
            d.pread([("P01", k % 2)], "V", qev[k])
            gslot = 512 * (k % 2)
            wv = {"T": qn_mmT[k - 2]} if k >= 2 else {}
            sqv_l[k] = d.v(wv, lambda e, k=k, gslot=gslot: e.tensor_tensor(
                d.gsc[:, gslot:gslot + 512],
                AR[:, QTB + 512 * k:QTB + 512 * k + 512],
                AR[:, QTB + 512 * k:QTB + 512 * k + 512], ALU.mult))

        def emit_qn(k):
            off = 512 * (k % 2)
            gslot = 512 * (k % 2)
            w2 = d.merge({"V": sqv_l[k]}, d.pwait([("P23", k % 2)]))
            qn_mmT[k] = d.t(w2, lambda e, gslot=gslot, off=off: e.matmul(
                d.P["P23"][0:1, off:off + 512], d.ones_f[:, :],
                d.gsc[:, gslot:gslot + 512], start=True, stop=True), inc=True)
            qn_red[k] = d.v({"T": qn_mmT[k]}, lambda e, k=k, off=off:
                            e.tensor_reduce(d.qnrow[0:1, k:k + 1],
                                            d.P["P23"][0:1, off:off + 512],
                                            AX.X, ALU.max))
            d.pread([("P23", k % 2)], "V", qn_red[k])

        for k in range(8):
            emit_qev(k)
            if k >= 1:
                emit_qn(k - 1)
        emit_qn(7)
        if self._pend_proj is not None:
            pb_, pblk_, pyl_ = self._pend_proj
            self._pend_proj = None
            self._emit_proj(pb_, pblk_, pyl_)
        d.v({"V": self.kmax_V}, lambda e: e.tensor_reduce(
            d.mrow[0:1, 1:2], d.qnrow[0:1, 0:8], AX.X, ALU.max))
        d.v({}, lambda e: e.tensor_tensor(d.mrow[0:1, 2:3], d.mrow[0:1, 1:2],
                                          d.mrow[0:1, 0:1], ALU.add))
        cvv = d.v({"G3": st.get("gcb_m2", 0)}, lambda e, cbslot=cbslot:
                  e.tensor_single_scalar(d.cn[cbslot][:, :], d.mrow[0:1, 2:3],
                                         -1.0 / 16.0, ALU.mult))
        wcb = {"V": cvv, "S": st.get("exp_m2", self.entS)}
        gcb = d.dg(wcb, lambda e, cbslot=cbslot: e.dma_start(
            out=d.Cb[:, cbslot:cbslot + 1],
            in_=bass.AP(d.cn[cbslot], 0, [[1, 1], [0, 128], [1, 1]])), q="G3")

        def vgen(p, extra_w):
            vaoff = VA + 2080 * (p % 2)
            pb = "P45" if p % 2 == 0 else "P67"
            offs = [0, 130, 260, 512, 642, 772]
            mts = {}
            evac = {}
            last = 0
            for ttile in range(16):
                sl = ttile % 6
                off = offs[sl]
                bank = sl // 3
                cyc = ttile // 6
                w = {}
                if sl in (0, 3):
                    if cyc == 0:
                        w = d.pwait([(pb, bank)])
                    else:
                        w = {"V": evac[2 * (cyc - 1) + bank]}
                if ttile == 0:
                    w = d.merge(w, {"V": self.ck_V}, extra_w)
                mts[ttile] = d.t(w, lambda e, p=p, ttile=ttile, off=off, pb=pb:
                                 e.matmul(
                    d.P[pb][:, off:off + 130],
                    AR[0:65, CKA + 128 * ttile:CKA + 128 * ttile + 128],
                    d.wvu_bf[:, 130 * p:130 * p + 130],
                    start=True, stop=True), inc=True)
                if sl in (2, 5) or ttile == 15:
                    n = (sl % 3) + 1
                    g0 = ttile - (sl % 3)
                    boff = offs[sl - (sl % 3)]
                    ev = d.v({"T": mts[ttile]}, lambda e, vaoff=vaoff, g0=g0,
                             n=n, boff=boff, pb=pb: e.tensor_copy(
                        AR[:, vaoff + 130 * g0:vaoff + 130 * (g0 + n)],
                        d.P[pb][:, boff:boff + 130 * n]))
                    evac[2 * cyc + bank] = ev
                    last = ev
            d.pread([(pb, 0)], "V", evac[4])
            d.pread([(pb, 1)], "V", last)
            return last

        vg_done = {0: vgen(0, {"T": st.get("attnv_6", 0)})}
        exp_S = {}
        attnv_T = {}

        def emit_scores(g):
            p, j = g // 16, g % 16
            pn = "P01" if g % 2 == 0 else "P23"
            w = d.pwait([(pn, 0), (pn, 1)])
            if g < 4:
                w = d.merge(w, {"V": max(qn_red[7], self.ck_V)})
            if j < 2:
                w = d.merge(w, {"V": qev[p]})
            d.t(w, lambda e, pn=pn, j=j, p=p: e.matmul(
                d.P[pn][:, 0:512],
                AR[0:64, CKV2 + 128 * j:CKV2 + 128 * j + 128],
                AR[0:64, QTB + 512 * p:QTB + 512 * p + 512],
                start=True, stop=True, tile_position=(0, 0)), inc=False)
            smT = d.t({}, lambda e, pn=pn, j=j, p=p: e.matmul(
                d.P[pn][:, 512:1024],
                AR[64:128, CKV2 + 128 * j:CKV2 + 128 * j + 128],
                AR[64:128, QTB + 512 * p:QTB + 512 * p + 512],
                start=True, stop=True, tile_position=(64, 0)), inc=True)
            we = {"T": max(smT * 0, smT), "G3": gcb}
            we["T"] = max(smT, attnv_T.get(g - 2, st.get("attnv_7", 0)))
            ss = d.s(we, lambda e, pn=pn, g=g, cbslot=cbslot: e.activation(
                AR[:, ETB + 1024 * (g % 2):ETB + 1024 * (g % 2) + 1024],
                d.P[pn][:, 0:1024], AF.Exp,
                bias=d.Cb[:, cbslot:cbslot + 1], scale=0.125))
            exp_S[g] = ss
            d.pread([(pn, 0), (pn, 1)], "S", ss)

        def emit_attnv(g):
            p, ja = g // 16, g % 16
            pb = "P45" if p % 2 == 0 else "P67"
            wa = {"S": exp_S[g]}
            if ja == 0:
                wa = d.merge(wa, {"V": vg_done[p]},
                             d.pwait([(pb, 0), (pb, 1)]))
            vaoff = VA + 2080 * (p % 2)
            d.t(wa, lambda e, ja=ja, vaoff=vaoff, pb=pb, g=g: e.matmul(
                d.P[pb][0:65, 0:512],
                AR[:, vaoff + 130 * ja:vaoff + 130 * ja + 65],
                AR[:, ETB + 1024 * (g % 2):ETB + 1024 * (g % 2) + 512],
                start=(ja == 0), stop=(ja == 15)), inc=False)
            at = d.t({}, lambda e, ja=ja, vaoff=vaoff, pb=pb, g=g: e.matmul(
                d.P[pb][0:65, 512:1024],
                AR[:, vaoff + 130 * ja + 65:vaoff + 130 * ja + 130],
                AR[:, ETB + 1024 * (g % 2) + 512:
                   ETB + 1024 * (g % 2) + 1024],
                start=(ja == 0), stop=(ja == 15)), inc=True)
            attnv_T[g] = at
            if ja == 15:
                emit_norm(p)

        def emit_norm(p):
            pb = "P45" if p % 2 == 0 else "P67"
            d.v({"T": attnv_T[16 * p + 15]}, lambda e, pb=pb: e.tensor_copy(
                d.outst[0:65, 0:512], d.P[pb][0:65, 0:512]))
            cpv = d.v({}, lambda e, pb=pb: e.tensor_copy(
                d.outst[0:65, 512:1024], d.P[pb][0:65, 512:1024]))
            d.pread([(pb, 0), (pb, 1)], "V", cpv)
            dg1 = d.dg({"V": cpv, "G0": self._bcg_prev},
                       lambda e: e.dma_start(
                out=d.R3[0:1, 0:1024], in_=d.outst[64:65, 0:1024]))
            bcg = d.dg({"G0": dg1}, lambda e: e.dma_start(
                out=d.gsc[0:64, 0:1024],
                in_=bass.AP(d.R3, 0, [[1, 1], [0, 64], [1, 1024]])))
            self._bcg_prev = bcg
            d.v({"G0": bcg}, lambda e: e.reciprocal(
                d.scr[0:64, 0:1024], d.gsc[0:64, 0:1024]))
            d.v({}, lambda e, p=p: e.tensor_tensor(
                AR[0:64, YT1 + 512 * p:YT1 + 512 * p + 512],
                d.outst[0:64, 0:512], d.scr[0:64, 0:512], ALU.mult))
            y2 = d.v({}, lambda e, p=p: e.tensor_tensor(
                AR[64:128, YT1 + 512 * p:YT1 + 512 * p + 512],
                d.outst[0:64, 512:1024], d.scr[0:64, 512:1024],
                ALU.mult))
            self._ylast = y2

        for g in range(130):
            if g >= 2:
                emit_attnv(g - 2)
            if g < 128:
                emit_scores(g)
            if g % 16 == 13 and g // 16 < 7:
                p = g // 16
                vg_done[p + 1] = vgen(
                    p + 1, {"T": attnv_T.get(16 * (p - 1) + 15,
                                             st.get("attnv_7", 0))})
        self._pend_proj = (b, blk, self._ylast)
        st["attnv_6"] = attnv_T[16 * 6 + 15]
        st["attnv_7"] = attnv_T[16 * 7 + 15]
        st["exp_m2"] = st.get("exp_m1", self.entS)
        st["exp_m1"] = exp_S[127]
        st["gcb_m2"] = st.get("gcb_m1", 0)
        st["gcb_m1"] = gcb

    def _emit_proj(self, b, blk, ylast):
        d = self
        AR = d.AR
        tq0 = 512 * blk
        slots = [("P01", 0), ("P01", 512), ("P23", 0), ("P23", 512)]
        for cc in range(8):
            pn, off = slots[cc % 4]
            pt = d.P[pn]
            for k in range(8):
                w = {}
                if k == 0:
                    w = d.pwait([(pn, off // 512)])
                    if cc == 0:
                        w = d.merge(w, {"V": ylast})
                d.t(w, lambda e, pt=pt, off=off, k=k, cc=cc: e.matmul(
                    pt[:, off:off + 512],
                    AR[:, WPC + 1024 * k + 128 * cc:
                       WPC + 1024 * k + 128 * cc + 128],
                    AR[:, YT1 + 512 * k:YT1 + 512 * k + 512],
                    start=(k == 0), stop=(k == 7)), inc=(k == 7))
            tt = d.cnt["T"]
            ev = d.v({"T": tt}, lambda e, pt=pt, off=off, cc=cc, tq0=tq0:
                     e.tensor_scalar(
                d.sT[:, 2048 * cc + tq0:2048 * cc + tq0 + 512],
                pt[:, off:off + 512], d.bpc[:, cc:cc + 1], None, ALU.add))
            d.pread([(pn, off // 512)], "V", ev)

    def ffn(self, b):
        d = self
        if self._pend_proj is not None:
            pb_, pblk_, pyl_ = self._pend_proj
            self._pend_proj = None
            self._emit_proj(pb_, pblk_, pyl_)
        AR = d.AR
        xrow = 1024 * b
        ffnT, ffnS = self.cnt["T"], self.cnt["S"]
        hp_v = {}
        silu_S = {}
        cv_v = {}
        cvg_v = {}
        gu_byi = {}
        prev_half_T = 0
        for half in range(2):
            for hc in range(24):
                i = half * 24 + hc
                par = i % 2
                ga_pn = "P01" if par == 0 else "P45"
                gb_pn = "P23" if par == 0 else "P67"
                if i == 0:
                    wq_ = {"V": max(self.entV, self.cvp_c.get(47, 0))}
                elif hc == 0:
                    wq_ = {"V": cvg_v[i - 1] if False else cv_v[i - 1]}
                else:
                    wq_ = {"S": cvg_v[i - 1]}
                wq2 = dict(wq_) if i == 0 or hc == 0 else {"S": cv_v[i - 1]}
                qg = d.dq(wq_, lambda e, hc=hc: e.dma_start(
                    out=bass.AP(d.IT, WFI,
                                [[2048, 128], [128, 8], [1, 128]]),
                    in_=bass.AP(d.d_wg.tensor, 128 * hc,
                                [[3072, 128], [393216, 8], [1, 128]])),
                          q="Q5")
                qu = d.dq(wq2, lambda e, hc=hc: e.dma_start(
                    out=bass.AP(d.IT, WFI + 1024,
                                [[2048, 128], [128, 8], [1, 128]]),
                    in_=bass.AP(d.d_wu.tensor, 128 * hc,
                                [[3072, 128], [393216, 8], [1, 128]])),
                          q="Q7")
                wcv = {"Q5": qg, "T": gu_byi.get(i - 2, 0)}
                if i == 0:
                    wcv = d.merge(wcv, {"T": ffnT, "S": ffnS})
                cvg_v[i] = d.s(wcv, lambda e, par=par: e.copy(
                    AR[:, WFB + 2048 * par:WFB + 2048 * par + 1024],
                    d.IT[:, WFI:WFI + 1024]))
                cv_v[i] = d.s({"Q7": qu}, lambda e, par=par: e.copy(
                    AR[:, WFB + 2048 * par + 1024:WFB + 2048 * par + 2048],
                    d.IT[:, WFI + 1024:WFI + 2048]))
                guT = {}
                for m in range(2):
                    pn = ga_pn if m == 0 else gb_pn
                    boff = 1024 * m
                    for tb in range(2):
                        w = {}
                        if m == 0 and tb == 0:
                            w = d.merge({"S": cvg_v[i]},
                                        d.pwait([(ga_pn, 0), (ga_pn, 1),
                                                 (gb_pn, 0), (gb_pn, 1)]))
                        if m == 1 and tb == 0:
                            w = {"S": cv_v[i]}
                        for c in range(8):
                            d.t(w if c == 0 else {}, lambda e, pn=pn, tb=tb,
                                c=c, par=par, boff=boff, half=half: e.matmul(
                                d.P[pn][:, 512 * tb:512 * tb + 512],
                                AR[:, WFB + 2048 * par + boff + 128 * c:
                                   WFB + 2048 * par + boff + 128 * c + 128],
                                d.xn[:, 2048 * c + 1024 * half + 512 * tb:
                                     2048 * c + 1024 * half + 512 * tb + 512],
                                start=(c == 0), stop=(c == 7)),
                                inc=(c == 7 and tb == 1))
                    guT[m] = d.cnt["T"]
                gu_byi[i] = guT[1]
                wsl = {"T": guT[0]}
                if i >= 2:
                    wsl["V"] = hp_v[i - 2]
                ssl = d.s(wsl, lambda e, ga_pn=ga_pn, par=par: e.activation(
                    AR[:, APR + 1024 * par:APR + 1024 * par + 1024],
                    d.P[ga_pn][:, 0:1024], AF.Silu, scale=S127))
                silu_S[i] = ssl
                d.pread([(ga_pn, 0), (ga_pn, 1)], "S", ssl)
                hv = d.v({"S": ssl, "T": guT[1]},
                         lambda e, gb_pn=gb_pn, par=par, hc=hc: e.tensor_tensor(
                    AR[:, HP + 1024 * hc:HP + 1024 * hc + 1024],
                    AR[:, APR + 1024 * par:APR + 1024 * par + 1024],
                    d.P[gb_pn][:, 0:1024], ALU.mult))
                hp_v[i] = hv
                d.pread([(gb_pn, 0), (gb_pn, 1)], "V", hv)
            dslots = [("P01", 0), ("P01", 512), ("P23", 0), ("P23", 512)]
            o2_v = {}
            dma_g = {}
            down_byc = {}
            wd_q = {}
            hlast = hp_v[half * 24 + 23]
            slast = silu_S[half * 24 + 23]
            for cc in range(8):
                par = cc % 2
                wl = {"T": max(down_byc.get(cc - 2, 0),
                               prev_half_T if cc < 2 else 0)}
                if cc < 2:
                    wl = d.merge(wl, {"V": hlast, "S": slast},
                                 self.wdb_done if half == 0 else {})
                wd_q[cc] = d.dq(wl, lambda e, cc=cc, par=par: e.dma_start(
                    out=bass.AP(d.AR, WDB + 3072 * par,
                                [[ACOLS, 128], [128, 24], [1, 128]]),
                    in_=bass.AP(d.d_wdb.tensor, 128 * cc,
                                [[1024, 128], [131072, 24], [1, 128]])),
                               q="Q6" if par == 0 else "Q4")
                self._wd_cv_last = 0
                for tb in range(2):
                    s4 = (2 * cc + tb) % 4
                    pn, off = dslots[s4]
                    pt = d.P[pn]
                    w = d.merge({"Q6" if par == 0 else "Q4": wd_q[cc]},
                                d.pwait([(pn, off // 512)]))
                    for hcc in range(24):
                        d.t(w if hcc == 0 else {}, lambda e, pt=pt, off=off,
                            hcc=hcc, par=par, tb=tb: e.matmul(
                            pt[:, off:off + 512],
                            AR[:, WDB + 3072 * par + 128 * hcc:
                               WDB + 3072 * par + 128 * hcc + 128],
                            AR[:, HP + 1024 * hcc + 512 * tb:
                               HP + 1024 * hcc + 512 * tb + 512],
                            start=(hcc == 0), stop=(hcc == 23)),
                            inc=(hcc == 23))
                    tt = d.cnt["T"]
                    down_byc[cc] = tt
                    s2 = (2 * cc + tb) % 2
                    wgs = {"T": tt, "V": o2_v.get(2 * cc + tb - 2, 0)}
                    gss = d.s(wgs, lambda e, pt=pt, off=off, s2=s2:
                              e.activation(d.gsc[:, 512 * s2:512 * s2 + 512],
                                           pt[:, off:off + 512], AF.Silu,
                                           scale=S127SQ))
                    d.pread([(pn, off // 512)], "S", gss)
                    xslot = (2 * cc + tb) % 4
                    qn_ = "Q%d" % xslot
                    lq = d.dq({"V": o2_v.get(2 * cc + tb - 4, 0)},
                              lambda e, cc=cc, half=half, tb=tb, xslot=xslot:
                              e.dma_start(
                        out=d.xs[:, 512 * xslot:512 * xslot + 512],
                        in_=d.d_x[xrow + 128 * cc:xrow + 128 * cc + 128,
                                  1024 * half + 512 * tb:
                                  1024 * half + 512 * tb + 512]), q=qn_)
                    d.v({"S": gss}, lambda e, s2=s2, cc=cc, half=half, tb=tb:
                        e.tensor_tensor(
                        d.outst[:, 0:512],
                        d.gsc[:, 512 * s2:512 * s2 + 512],
                        d.sT[:, 2048 * cc + 1024 * half + 512 * tb:
                             2048 * cc + 1024 * half + 512 * tb + 512],
                        ALU.mult))
                    gq = "G%d" % (1 + s2)
                    ov = d.v({qn_: lq, gq: dma_g.get(2 * cc + tb - 2, 0)},
                             lambda e, s2=s2, xslot=xslot: e.tensor_tensor(
                        d.outst[:, 512 + 512 * s2:512 + 512 * s2 + 512],
                        d.outst[:, 0:512],
                        d.xs[:, 512 * xslot:512 * xslot + 512], ALU.add))
                    o2_v[2 * cc + tb] = ov
                    dma_g[2 * cc + tb] = d.dg(
                        {"V": ov}, lambda e, cc=cc, half=half, tb=tb, s2=s2:
                        e.dma_start(
                            out=d.d_out[xrow + 128 * cc:xrow + 128 * cc + 128,
                                        1024 * half + 512 * tb:
                                        1024 * half + 512 * tb + 512],
                            in_=d.outst[:, 512 + 512 * s2:
                                        512 + 512 * s2 + 512]), q=gq)
            prev_half_T = d.cnt["T"]


def kernel(**inputs):
    f32 = np.float32
    x = np.asarray(inputs["L0_attn_x"], f32)
    wq = np.ascontiguousarray(np.asarray(inputs["L0_attn_wq_d"], f32))
    bq = np.ascontiguousarray(np.asarray(inputs["L0_attn_bq_d"], f32).reshape(512, 1))
    wqk = np.asarray(inputs["L0_attn_w_qk"], f32)
    bqk = np.ascontiguousarray(np.asarray(inputs["L0_attn_b_qk"], f32).reshape(1024, 1))
    wkv = np.asarray(inputs["L0_attn_wkv_d"], f32)
    bkv = np.ascontiguousarray(np.asarray(inputs["L0_attn_bkv_d"], f32).reshape(64, 1))
    wvu = np.asarray(inputs["L0_attn_wv_u"], f32)
    bvu = np.asarray(inputs["L0_attn_bv_u"], f32)
    wp = np.asarray(inputs["L0_attn_w_proj"], f32)
    bp = np.ascontiguousarray(np.asarray(inputs["L0_attn_b_proj"], f32).reshape(1024, 1))
    wg = np.asarray(inputs["L0_ffn_w_gate"], np.int32)
    wu = np.asarray(inputs["L0_ffn_w_up"], np.int32)
    wd = np.asarray(inputs["L0_ffn_w_down"], np.int32)

    wqkT = np.ascontiguousarray(wqk.T)
    wkvT = np.ascontiguousarray(wkv.T)
    wpT = np.ascontiguousarray(wp.T)
    wgT = np.ascontiguousarray(wg.T)
    wuT = np.ascontiguousarray(wu.T)
    wdT = np.ascontiguousarray(wd.T)
    wvuT = wvu.T
    wvu_aug = np.zeros((65, 1040), f32)
    for h in range(16):
        wvu_aug[0:64, 65 * h:65 * h + 64] = wvuT[:, 64 * h:64 * h + 64]
        wvu_aug[64, 65 * h:65 * h + 64] = bvu[64 * h:64 * h + 64]
        wvu_aug[64, 65 * h + 64] = 1.0
    wvu_aug = np.ascontiguousarray(wvu_aug)

    nc = KB().build()
    in_maps = []
    for c in range(8):
        xsh = np.ascontiguousarray(
            x[2 * c:2 * c + 2].transpose(0, 2, 1)).reshape(2048, 2048)
        in_maps.append({
            "xT": xsh, "wq": wq, "wqkT": wqkT, "bq": bq, "bqk": bqk,
            "wkvT": wkvT, "bkv": bkv, "wvu": wvu_aug, "wpT": wpT, "bp": bp,
            "wgT": wgT, "wuT": wuT, "wdT": wdT,
        })
    res = run_bass_kernel_spmd(nc, in_maps, core_ids=list(range(8)))
    out = np.empty((16, 2048, 1024), f32)
    for c in range(8):
        o = np.asarray(res.results[c]["outT"]).reshape(2, 1024, 2048)
        out[2 * c:2 * c + 2] = o.transpose(0, 2, 1)
    return out


if __name__ == "__main__":
    KB().build()
    print("build ok")


# version 44
# speedup vs baseline: 1.2378x; 1.1410x over previous
import contextlib
import numpy as np

import concourse.bass as bass
import concourse.mybir as mybir
from concourse.bass_utils import run_bass_kernel_spmd

F32 = mybir.dt.float32
BF16 = mybir.dt.bfloat16
I32 = mybir.dt.int32
AF = mybir.ActivationFunctionType
ALU = mybir.AluOpType
AX = mybir.AxisListType

T = 2048
C = 1024
H3 = 3072
S127 = float(np.float32(1.0 / 127.0))
S127SQ = float(np.float32(1.0 / 127.0)) ** 2

WC = 0
CKA = 8192
CKV2 = 10240
VA = 12288
QTB = 16448
ETB = 20544
YT1 = 22592
WPC = 26688
ACOLS = 34880
ABC = 12288
BBC = 14336
WQB = 16448
WQKB = 20544
HP = 0
WFB = 24576
APR = 28672
WDB = 28672

WFI = 0
WDI = 0


class KB:
    def __init__(self):
        self.nc = bass.Bass("TRN2", target_bir_lowering=False, debug=False)
        self.cnt = {"T": 0, "V": 0, "S": 0}
        for i in range(8):
            self.cnt["Q%d" % i] = 0
        for i in range(4):
            self.cnt["G%d" % i] = 0
        self.q = {"tensor": [], "vector": [], "scalar": [], "sync": [],
                  "gpsimd": []}
        self.pr = {}

    def flush(self):
        for eng in list(self.q):
            lst = self.q[eng]
            if lst:
                def body(h, lst=lst):
                    for f in lst:
                        f(h)
                getattr(self.block, eng)(body)
            self.q[eng] = []

    def _w(self, e, waits):
        for k, v in waits.items():
            if v and v > 0:
                e.wait_ge(self.sem[k],
                          v * 16 if k[0] in ("Q", "G") else v)

    def _mk(self, eng, waits, fn, inc_key=None):
        w = dict(waits)
        sem = self.sem[inc_key] if inc_key else None
        amt = 16 if (inc_key and inc_key[0] in ("Q", "G")) else 1
        def g(e, w=w, fn=fn, sem=sem, amt=amt):
            self._w(e, w)
            r = fn(e)
            if sem is not None:
                r.then_inc(sem, amt)
        self.q[eng].append(g)
        if inc_key:
            self.cnt[inc_key] += 1
            return self.cnt[inc_key]
        return None

    def t(self, waits, fn, inc=False):
        r = self._mk("tensor", waits, fn, "T" if inc else None)
        return r if r is not None else self.cnt["T"]

    def v(self, waits, fn):
        return self._mk("vector", waits, fn, "V")

    def s(self, waits, fn):
        return self._mk("scalar", waits, fn, "S")

    def dq(self, waits, fn, q="Q7"):
        return self._mk("sync", waits, fn, q)

    def dg(self, waits, fn, q="G0"):
        return self._mk("gpsimd", waits, fn, q)

    def pread(self, keys, eng, val):
        for key in keys:
            d = self.pr.setdefault(key, {})
            d[eng] = max(d.get(eng, 0), val)

    def pwait(self, keys):
        out = {}
        for key in keys:
            for eng, v in self.pr.get(key, {}).items():
                out[eng] = max(out.get(eng, 0), v)
        return out

    @staticmethod
    def merge(*ds):
        out = {}
        for d in ds:
            for k, v in d.items():
                out[k] = max(out.get(k, 0), v)
        return out

    def bc_src(self, tens, cols):
        return bass.AP(tens, 0, [[1, 1], [0, 128], [1, cols]])

    def build(self):
        nc = self.nc
        self.d_x = nc.dram_tensor("xT", [2 * C, T], F32, kind="ExternalInput").ap()
        self.d_wq = nc.dram_tensor("wq", [512, C], F32, kind="ExternalInput").ap()
        self.d_wqkT = nc.dram_tensor("wqkT", [512, C], F32, kind="ExternalInput").ap()
        self.d_bq = nc.dram_tensor("bq", [512, 1], F32, kind="ExternalInput").ap()
        self.d_bqk = nc.dram_tensor("bqk", [C, 1], F32, kind="ExternalInput").ap()
        self.d_wkvT = nc.dram_tensor("wkvT", [C, 64], F32, kind="ExternalInput").ap()
        self.d_bkv = nc.dram_tensor("bkv", [64, 1], F32, kind="ExternalInput").ap()
        self.d_wvu = nc.dram_tensor("wvu", [65, 1040], F32, kind="ExternalInput").ap()
        self.d_wpT = nc.dram_tensor("wpT", [C, C], F32, kind="ExternalInput").ap()
        self.d_bp = nc.dram_tensor("bp", [C, 1], F32, kind="ExternalInput").ap()
        self.d_wg = nc.dram_tensor("wgT", [C, H3], I32, kind="ExternalInput").ap()
        self.d_wu = nc.dram_tensor("wuT", [C, H3], I32, kind="ExternalInput").ap()
        self.d_wd = nc.dram_tensor("wdT", [H3, C], I32, kind="ExternalInput").ap()
        self.d_out = nc.dram_tensor("outT", [2 * C, T], F32,
                                    kind="ExternalOutput").ap()
        self.d_wdb = nc.dram_tensor("wdB", [H3, C], BF16, kind="Internal").ap()

        with contextlib.ExitStack() as es:
            sb = lambda n, s, d: es.enter_context(nc.sbuf_tensor(n, s, d))
            self.AR = sb("AR", [128, ACOLS], BF16)
            self.IT = sb("IT", [128, 2048], I32)
            self.xn = sb("xn", [128, 16384], BF16)
            self.sT = sb("sTb", [128, 16384], BF16)
            self.wkv_bf = sb("wkv_bf", [128, 512], BF16)
            self.wvu_bf = sb("wvu_bf", [65, 1040], BF16)
            self.xs = sb("xs", [128, 2048], F32)
            self.CONVST = sb("CONVST", [128, 1024], BF16)
            self.scr = sb("scr", [128, 1024], F32)
            self.outst = sb("outst", [128, 1536], F32)
            self.gsc = sb("gsc", [128, 1024], F32)
            self.R1 = sb("R1", [1, 2048], F32)
            self.R2 = sb("R2", [1, 2048], F32)
            self.R3 = sb("R3", [1, 2048], F32)
            self.R4 = sb("R4", [1, 2048], F32)
            self.arow_bf = sb("arow_bf", [1, 2048], BF16)
            self.brow_bf = sb("brow_bf", [1, 2048], BF16)
            self.bqc = sb("bqc", [128, 4], F32)
            self.bqc_bf = sb("bqc_bf", [128, 4], BF16)
            self.bqkc = sb("bqkc", [128, 8], F32)
            self.bkv_sb = sb("bkv_sb", [64, 1], F32)
            self.bpc = sb("bpc", [128, 8], F32)
            self.bcomb = sb("bcomb", [128, 8], F32)
            self.ones_f = sb("ones_f", [128, 1], F32)
            self.onesr = sb("onesr", [1, 64], F32)
            self.qnrow = sb("qnrow", [1, 16], F32)
            self.mrow = sb("mrow", [1, 16], F32)
            self.cn = [sb("cn0", [1, 1], F32), sb("cn1", [1, 1], F32)]
            self.Cb = sb("Cb", [128, 2], F32)
            self.P = {}
            for nm in ("P01", "P23", "P45", "P67"):
                self.P[nm] = es.enter_context(nc.psum_tensor(nm, [128, 1024], F32))
            self.sem = {}
            for k in (["T", "V", "S"] + ["Q%d" % i for i in range(8)]
                      + ["G%d" % i for i in range(4)]):
                self.sem[k] = es.enter_context(nc.semaphore("SEM" + k))
            self.block = es.enter_context(nc.Block())

            self.init_phase()
            self.flush()
            self.entT, self.entV, self.entS = self.cnt["T"], self.cnt["V"], 0
            self.entG1, self.entG2 = self.cnt["G1"], self.cnt["G2"]
            self.entG0 = self.cnt["G0"]
            for b in range(2):
                self.fold(b)
                self.flush()
                self.ln(b)
                self.flush()
                self.qkv(b)
                self.flush()
                self.st = {"exp_m1": self.entS}
                for blk in range(4):
                    self.attn_blk(b, blk)
                    self.flush()
                self.ffn(b)
                self.flush()
                self.entT, self.entV, self.entS = (self.cnt["T"], self.cnt["V"],
                                                   self.cnt["S"])
                self.entG0 = self.cnt["G0"]
                self.entG1, self.entG2 = self.cnt["G1"], self.cnt["G2"]
            tot = {k: self.cnt[k] for k in self.cnt if k[0] in ("Q", "G")}
            def fin(e, tot=tot):
                for k, v in tot.items():
                    if v:
                        e.wait_ge(self.sem[k], 16 * v)
            self.q["gpsimd"].append(fin)
            self.flush()
        return nc

    def init_phase(self):
        d = self
        for k in range(4):
            d.dq({}, lambda e, k=k: e.dma_start(
                out=d.bqc[:, k:k + 1], in_=d.d_bq[128 * k:128 * k + 128, :]))
        for k in range(8):
            d.dq({}, lambda e, k=k: e.dma_start(
                out=d.bqkc[:, k:k + 1], in_=d.d_bqk[128 * k:128 * k + 128, :]))
        d.dq({}, lambda e: e.dma_start(out=d.bkv_sb[:, :], in_=d.d_bkv))
        for k in range(8):
            d.dq({}, lambda e, k=k: e.dma_start(
                out=d.bpc[:, k:k + 1], in_=d.d_bp[128 * k:128 * k + 128, :]))
        for k in range(8):
            d.dq({}, lambda e, k=k: e.dma_start(
                out=d.scr[:, 64 * k:64 * k + 64],
                in_=d.d_wkvT[128 * k:128 * k + 128, :]))
        q1 = d.dq({}, lambda e: e.dma_start(out=d.xs[0:65, 0:1040], in_=d.d_wvu))
        d.v({"Q7": q1}, lambda e: e.tensor_copy(d.wkv_bf[:, :], d.scr[:, 0:512]))
        d.v({}, lambda e: e.tensor_copy(d.bqc_bf[:, :], d.bqc[:, :]))
        d.v({}, lambda e: e.tensor_copy(d.wvu_bf[:, :], d.xs[0:65, 0:1040]))
        d.v({}, lambda e: e.memset(d.ones_f[:, :], 1.0))
        d.v({}, lambda e: e.memset(d.onesr[:, :], 1.0))
        self.cvp_c = {}
        self.stp_c = {}
        self._bcg_prev = 0
        self._pend_proj = None
        self._rv_buf = {}
        self._dg1_prev = 0
        self._ylast_final = 0
        self.wdb_done = {}
        self.v_init = d.cnt["V"]
        self.entV = self.v_init

    def conv_piece(self, pc):
        d = self
        r0 = 128 * (pc // 2)
        c0 = 512 * (pc % 2)
        par = pc % 4
        sl = pc % 2
        qn_ = "Q5" if sl == 0 else "Q7"
        gq = "G1" if sl == 0 else "G2"
        lq = d.dq({"V": d.cvp_c.get(pc - 4, 0)},
                  lambda e, r0=r0, c0=c0, par=par:
                  e.dma_start(out=d.IT[:, 512 * par:512 * par + 512],
                              in_=d.d_wd[r0:r0 + 128, c0:c0 + 512]), q=qn_)
        wc = {qn_: lq, gq: d.stp_c.get(pc - 2, 0)}
        d.cvp_c[pc] = d.v(wc, lambda e, par=par, sl=sl: e.tensor_copy(
            d.CONVST[:, 512 * sl:512 * sl + 512],
            d.IT[:, 512 * par:512 * par + 512]))
        d.stp_c[pc] = d.dg({"V": d.cvp_c[pc]},
                           lambda e, r0=r0, c0=c0, sl=sl:
                           e.dma_start(out=d.d_wdb[r0:r0 + 128, c0:c0 + 512],
                                       in_=d.CONVST[:, 512 * sl:512 * sl + 512]),
                           q=gq)
        if pc == 47:
            self.wdb_done = {"G1": d.stp_c[46], "G2": d.stp_c[47]}

    def fold(self, b):
        d = self
        AR = d.AR
        entT, entV, entS = self.entT, self.entV, self.entS
        cv = []
        for n in range(8):
            src = d.d_wq if n < 4 else d.d_wqkT
            k = n % 4
            dst_off = (WQB if n < 4 else WQKB) + 1024 * k
            stage = d.scr[:, :] if n % 2 == 0 else d.xs[:, 0:1024]
            qn_ = "Q4" if n % 2 == 0 else "Q2"
            wv = {"V": cv[n - 2] if n >= 2 else entV}
            qq = d.dq(wv, lambda e, src=src, k=k, stage=stage: e.dma_start(
                out=stage, in_=src[128 * k:128 * k + 128, :]), q=qn_)
            wv2 = {qn_: qq}
            if n == 0:
                wv2["T"] = entT
                wv2["S"] = entS
            cv.append(d.v(wv2, lambda e, dst_off=dst_off, stage=stage:
                          e.tensor_copy(AR[:, dst_off:dst_off + 1024],
                                        stage)))
        conv_done = cv[-1]
        wq_done = cv[3]

        slots = [("P01", 0), ("P01", 512), ("P23", 0), ("P23", 512)]
        wc_ev = []
        n = 0
        for i in range(8):
            for j2 in range(2):
                pn, off = slots[n % 4]
                pt = d.P[pn]
                for ql in range(4):
                    w = {}
                    if ql == 0:
                        w["V"] = wc_ev[n - 4] if n >= 4 else conv_done
                        w = d.merge(w, d.pwait([(pn, off // 512)]))
                    d.t(w, lambda e, pt=pt, off=off, ql=ql, i=i, j2=j2: e.matmul(
                        pt[:, off:off + 512],
                        AR[:, WQB + 1024 * ql + 128 * i:
                           WQB + 1024 * ql + 128 * i + 128],
                        AR[:, WQKB + 1024 * ql + 512 * j2:
                           WQKB + 1024 * ql + 512 * j2 + 512],
                        start=(ql == 0), stop=(ql == 3)), inc=(ql == 3))
                tt = d.cnt["T"]
                wev = {"T": tt}
                ev = d.v(wev, lambda e, pt=pt, off=off, i=i, j2=j2:
                         e.tensor_copy(AR[:, WC + 1024 * i + 512 * j2:
                                          WC + 1024 * i + 512 * j2 + 512],
                                       pt[:, off:off + 512]))
                d.pread([(pn, off // 512)], "V", ev)
                wc_ev.append(ev)
                n += 1
        for j in range(8):
            off = 512 * (j % 2)
            for ql in range(4):
                w = {}
                if ql == 0:
                    w = d.pwait([("P67", off // 512)])
                    if j == 0:
                        w = d.merge(w, {"V": wq_done})
                d.t(w, lambda e, j=j, ql=ql, off=off: e.matmul(
                    d.P["P67"][:, off:off + 1],
                    AR[:, WQKB + 1024 * ql + 128 * j:
                       WQKB + 1024 * ql + 128 * j + 128],
                    d.bqc_bf[:, ql:ql + 1], start=(ql == 0), stop=(ql == 3)),
                    inc=(ql == 3))
            tt = d.cnt["T"]
            ev = d.v({"T": tt}, lambda e, j=j, off=off: e.tensor_tensor(
                d.bcomb[:, j:j + 1], d.P["P67"][:, off:off + 1],
                d.bqkc[:, j:j + 1], ALU.add))
            d.pread([("P67", off // 512)], "V", ev)
        self.fold_T = d.cnt["T"]
        self.fold_V = d.cnt["V"]

    def ln(self, b):
        d = self
        AR = d.AR
        xrow = 1024 * b
        prevT = 0
        prevV = self.fold_V
        for half in range(2):
            mmT = [0] * 8
            sq_v = [0] * 8
            for k in range(8):
                slot = 1024 * (k % 2)
                wv = ({"T": mmT[k - 2], "V": sq_v[k - 2]} if k >= 2 else
                      {"T": prevT, "V": prevV})
                qn_ = "Q0" if k % 2 == 0 else "Q1"
                ld = d.dq(wv, lambda e, k=k, half=half, slot=slot: e.dma_start(
                    out=d.xs[:, slot:slot + 1024],
                    in_=d.d_x[xrow + 128 * k:xrow + 128 * k + 128,
                              1024 * half:1024 * half + 1024]), q=qn_)
                sqs = d.gsc if k % 2 == 0 else d.outst
                wsq = {qn_: ld}
                if k >= 2:
                    wsq["T"] = mmT[k - 2]
                if half == 0 and k <= 1:
                    wsq["G1"] = self.entG1
                    wsq["G2"] = self.entG2
                sq_v[k] = d.v(wsq, lambda e, slot=slot, sqs=sqs: e.tensor_tensor(
                    sqs[:, 0:1024], d.xs[:, slot:slot + 1024],
                    d.xs[:, slot:slot + 1024], ALU.mult))
                w0 = {qn_: ld}
                if k == 0:
                    w0 = d.merge(w0, d.pwait([("P45", 0), ("P45", 1)]))
                for nb in range(2):
                    d.t(w0 if nb == 0 else {}, lambda e, slot=slot, nb=nb, k=k:
                        e.matmul(d.P["P45"][0:1, 512 * nb:512 * nb + 512],
                                 d.ones_f[:, :],
                                 d.xs[:, slot + 512 * nb:slot + 512 * nb + 512],
                                 start=(k == 0), stop=(k == 7)), inc=False)
                wsq2 = {"V": sq_v[k]}
                if k == 0:
                    wsq2 = d.merge(wsq2, d.pwait([("P67", 0), ("P67", 1)]))
                for nb in range(2):
                    d.t(wsq2 if nb == 0 else {}, lambda e, sqs=sqs, nb=nb, k=k:
                        e.matmul(d.P["P67"][0:1, 512 * nb:512 * nb + 512],
                                 d.ones_f[:, :],
                                 sqs[:, 512 * nb:512 * nb + 512],
                                 start=(k == 0), stop=(k == 7)),
                        inc=(nb == 1))
                mmT[k] = d.cnt["T"]
            tt = d.cnt["T"]
            for nb in range(2):
                ev = d.v({"T": tt}, lambda e, half=half, nb=nb: e.tensor_copy(
                    d.R1[0:1, 1024 * half + 512 * nb:
                         1024 * half + 512 * nb + 512],
                    d.P["P45"][0:1, 512 * nb:512 * nb + 512]))
                d.pread([("P45", nb)], "V", ev)
                ev = d.v({}, lambda e, half=half, nb=nb: e.tensor_copy(
                    d.R2[0:1, 1024 * half + 512 * nb:
                         1024 * half + 512 * nb + 512],
                    d.P["P67"][0:1, 512 * nb:512 * nb + 512]))
                d.pread([("P67", nb)], "V", ev)
                prevV = ev
            prevT = tt
        d.v({"T": self.entT, "G0": self.entG0}, lambda e: e.tensor_scalar_mul(
            d.R3[:, :], d.R1[:, :], 1.0 / 1024.0))
        d.v({}, lambda e: e.tensor_scalar_mul(d.R4[:, :], d.R2[:, :],
                                              1.0 / 1024.0))
        d.v({}, lambda e: e.tensor_tensor(d.R1[:, :], d.R3[:, :],
                                          d.R3[:, :], ALU.mult))
        d.v({}, lambda e: e.tensor_tensor(d.R2[:, :], d.R4[:, :],
                                          d.R1[:, :], ALU.subtract))
        vv = d.v({}, lambda e: e.tensor_scalar_add(d.R4[:, :], d.R2[:, :],
                                                   1e-5))
        ss = d.s({"V": vv}, lambda e: e.activation(d.R2[:, :], d.R4[:, :],
                                                   AF.Sqrt))
        d.v({"S": ss}, lambda e: e.reciprocal(d.R4[:, :], d.R2[:, :]))
        d.v({}, lambda e: e.scalar_tensor_tensor(
            d.R1[:, :], d.R3[:, :], -1.0, d.R4[:, :], ALU.mult, ALU.mult))
        d.v({}, lambda e: e.tensor_copy(d.arow_bf[:, :], d.R4[:, :]))
        vv = d.v({}, lambda e: e.tensor_copy(d.brow_bf[:, :], d.R1[:, :]))
        g1 = d.dg({"V": vv, "T": self.entT, "S": self.entS},
                  lambda e: e.dma_start(out=AR[:, ABC:ABC + 2048],
                                        in_=d.bc_src(d.arow_bf, 2048)))
        g2 = d.dg({}, lambda e: e.dma_start(out=AR[:, BBC:BBC + 2048],
                                            in_=d.bc_src(d.brow_bf, 2048)))
        xv = [0] * 16
        for half in range(2):
            for k in range(8):
                i = half * 8 + k
                slot = 1024 * (i % 2)
                wv = {"V": xv[i - 2]} if i >= 2 else {"V": vv}
                qn_ = "Q0" if i % 2 == 0 else "Q1"
                ld = d.dq(wv, lambda e, k=k, half=half, slot=slot: e.dma_start(
                    out=d.xs[:, slot:slot + 1024],
                    in_=d.d_x[xrow + 128 * k:xrow + 128 * k + 128,
                              1024 * half:1024 * half + 1024]), q=qn_)
                tmp = d.gsc if i % 2 == 0 else d.outst
                w1 = {qn_: ld, "G0": g2}
                if i == 0:
                    w1["T"] = self.entT
                d.v(w1, lambda e, slot=slot, tmp=tmp, half=half: e.tensor_tensor(
                    tmp[:, 0:1024], d.xs[:, slot:slot + 1024],
                    AR[:, ABC + 1024 * half:ABC + 1024 * half + 1024], ALU.mult))
                xv[i] = d.v({}, lambda e, tmp=tmp, half=half, k=k:
                            e.tensor_tensor(
                    d.xn[:, 2048 * k + 1024 * half:
                         2048 * k + 1024 * half + 1024],
                    tmp[:, 0:1024],
                    AR[:, BBC + 1024 * half:BBC + 1024 * half + 1024], ALU.add))
        self.ln_V = d.cnt["V"]

    def qkv(self, b):
        d = self
        AR = d.AR
        m1 = d.v({"T": self.entT, "S": self.entS},
                 lambda e: e.memset(AR[64:65, CKA:CKA + 2048], 1.0))
        ck_ev = [0] * 4
        for nb in range(4):
            off = 512 * (nb % 2)
            for c in range(8):
                w = {}
                if c == 0:
                    w = d.pwait([("P45", nb % 2)])
                    w = d.merge(w, {"V": self.ln_V if nb < 2 else ck_ev[nb - 2]})
                d.t(w, lambda e, off=off, c=c, nb=nb: e.matmul(
                    d.P["P45"][0:64, off:off + 512],
                    d.wkv_bf[:, 64 * c:64 * c + 64],
                    d.xn[:, 2048 * c + 512 * nb:2048 * c + 512 * nb + 512],
                    start=(c == 0), stop=(c == 7)), inc=(c == 7))
            tt = d.cnt["T"]
            ev1 = d.v({"T": tt, "V": m1}, lambda e, off=off, nb=nb:
                      e.tensor_scalar(
                          AR[0:64, CKA + 512 * nb:CKA + 512 * nb + 512],
                          d.P["P45"][0:64, off:off + 512], d.bkv_sb[:, :], None,
                          ALU.add))
            d.pread([("P45", nb % 2)], "V", ev1)
            d.v({}, lambda e, nb=nb: e.tensor_copy(
                AR[0:64, CKV2 + 512 * nb:CKV2 + 512 * nb + 512],
                AR[0:64, CKA + 512 * nb:CKA + 512 * nb + 512]))
            ck_ev[nb] = d.v({}, lambda e, nb=nb: e.tensor_copy(
                AR[64:128, CKV2 + 512 * nb:CKV2 + 512 * nb + 512],
                AR[0:64, CKA + 512 * nb:CKA + 512 * nb + 512]))
        k2mmT = [0] * 4
        for nb in range(4):
            off = 512 * (nb % 2)
            wsq = {}
            if nb >= 1:
                wsq["T"] = k2mmT[nb - 1]
            sqv = d.v(wsq, lambda e, nb=nb: e.tensor_tensor(
                d.gsc[0:64, 0:512],
                AR[0:64, CKV2 + 512 * nb:CKV2 + 512 * nb + 512],
                AR[0:64, CKV2 + 512 * nb:CKV2 + 512 * nb + 512], ALU.mult))
            w = d.merge({"V": sqv}, d.pwait([("P67", nb % 2)]))
            k2mmT[nb] = d.t(w, lambda e, off=off: e.matmul(
                d.P["P67"][0:1, off:off + 512], d.ones_f[0:64, :],
                d.gsc[0:64, 0:512], start=True, stop=True), inc=True)
            ev = d.v({"T": k2mmT[nb]}, lambda e, off=off, nb=nb: e.tensor_copy(
                d.R4[0:1, 512 * nb:512 * nb + 512],
                d.P["P67"][0:1, off:off + 512]))
            d.pread([("P67", nb % 2)], "V", ev)
        d.v({}, lambda e: e.tensor_reduce(d.mrow[0:1, 0:1], d.R4[:, :], AX.X,
                                          ALU.max))
        self.kmax_V = d.cnt["V"]
        wp_cv = []
        for k in range(8):
            stage = d.scr[:, :] if k % 2 == 0 else d.xs[:, 0:1024]
            qn_ = "Q4" if k % 2 == 0 else "Q2"
            wv = {"V": wp_cv[k - 2]} if k >= 2 else {"V": self.ln_V}
            qq = d.dq(wv, lambda e, k=k, stage=stage: e.dma_start(
                out=stage, in_=d.d_wpT[128 * k:128 * k + 128, :]), q=qn_)
            w2 = {qn_: qq}
            if k == 0:
                w2["T"] = self.entT
                w2["S"] = self.entS
            wp_cv.append(d.v(w2, lambda e, k=k, stage=stage: e.tensor_copy(
                AR[:, WPC + 1024 * k:WPC + 1024 * k + 1024], stage)))
        self.wp_V = d.cnt["V"]
        self.ck_V = ck_ev[3]

    def attn_blk(self, b, blk):
        d = self
        AR = d.AR
        st = self.st
        tq0 = 512 * blk
        cbslot = blk % 2
        conv_left = list(range(12 * blk, 12 * blk + 12)) if b == 0 else []
        qev = [0] * 8
        qn_red = [0] * 8
        qn_mmT = [0] * 8
        sqv_l = [0] * 8

        def emit_qev(k):
            off = 512 * (k % 2)
            if k < 2:
                w = d.merge(d.pwait([("P01", k % 2)]), {"V": self.wp_V})
            else:
                w = {"V": qev[k - 2]}
            for c in range(8):
                d.t(w if c == 0 else {}, lambda e, off=off, c=c, k=k, tq0=tq0:
                    e.matmul(
                    d.P["P01"][:, off:off + 512],
                    AR[:, WC + 1024 * c + 128 * k:WC + 1024 * c + 128 * k + 128],
                    d.xn[:, 2048 * c + tq0:2048 * c + tq0 + 512],
                    start=(c == 0), stop=(c == 7)), inc=(c == 7))
            tt = d.cnt["T"]
            qev[k] = d.v({"T": tt}, lambda e, off=off, k=k: e.tensor_scalar(
                AR[:, QTB + 512 * k:QTB + 512 * k + 512],
                d.P["P01"][:, off:off + 512], d.bcomb[:, k:k + 1], None,
                ALU.add))
            d.pread([("P01", k % 2)], "V", qev[k])
            gslot = 512 * (k % 2)
            wv = {"T": qn_mmT[k - 2]} if k >= 2 else {}
            sqv_l[k] = d.v(wv, lambda e, k=k, gslot=gslot: e.tensor_tensor(
                d.gsc[:, gslot:gslot + 512],
                AR[:, QTB + 512 * k:QTB + 512 * k + 512],
                AR[:, QTB + 512 * k:QTB + 512 * k + 512], ALU.mult))

        def emit_qn(k):
            off = 512 * (k % 2)
            gslot = 512 * (k % 2)
            w2 = d.merge({"V": sqv_l[k]}, d.pwait([("P23", k % 2)]))
            qn_mmT[k] = d.t(w2, lambda e, gslot=gslot, off=off: e.matmul(
                d.P["P23"][0:1, off:off + 512], d.ones_f[:, :],
                d.gsc[:, gslot:gslot + 512], start=True, stop=True), inc=True)
            qn_red[k] = d.v({"T": qn_mmT[k]}, lambda e, k=k, off=off:
                            e.tensor_reduce(d.qnrow[0:1, k:k + 1],
                                            d.P["P23"][0:1, off:off + 512],
                                            AX.X, ALU.max))
            d.pread([("P23", k % 2)], "V", qn_red[k])

        for k in range(8):
            emit_qev(k)
            if k >= 1:
                emit_qn(k - 1)
        emit_qn(7)
        if self._pend_proj is not None:
            pb_, pblk_, fin_ = self._pend_proj
            self._pend_proj = None
            fin_(7)
            self._emit_proj(pb_, pblk_, self._ylast)
        d.v({"V": self.kmax_V}, lambda e: e.tensor_reduce(
            d.mrow[0:1, 1:2], d.qnrow[0:1, 0:8], AX.X, ALU.max))
        d.v({}, lambda e: e.tensor_tensor(d.mrow[0:1, 2:3], d.mrow[0:1, 1:2],
                                          d.mrow[0:1, 0:1], ALU.add))
        cvv = d.v({"G3": st.get("gcb_m2", 0)}, lambda e, cbslot=cbslot:
                  e.tensor_single_scalar(d.cn[cbslot][:, :], d.mrow[0:1, 2:3],
                                         -1.0 / 16.0, ALU.mult))
        wcb = {"V": cvv, "S": st.get("exp_m2", self.entS)}
        gcb = d.dg(wcb, lambda e, cbslot=cbslot: e.dma_start(
            out=d.Cb[:, cbslot:cbslot + 1],
            in_=bass.AP(d.cn[cbslot], 0, [[1, 1], [0, 128], [1, 1]])), q="G3")

        vst = {}

        def vgen_part(p, tlo, thi, extra_w):
            vaoff = VA + 2080 * (p % 2)
            pb = "P45" if p % 2 == 0 else "P67"
            offs = [0, 130, 260, 512, 642, 772]
            s_ = vst.setdefault(p, {"mts": {}, "evac": {}, "last": 0})
            for ttile in range(tlo, thi):
                sl = ttile % 6
                off = offs[sl]
                bank = sl // 3
                cyc = ttile // 6
                w = {}
                if sl in (0, 3):
                    if cyc == 0:
                        w = d.pwait([(pb, bank)])
                    else:
                        w = {"V": s_["evac"][2 * (cyc - 1) + bank]}
                if ttile == 0:
                    w = d.merge(w, {"V": self.ck_V}, extra_w)
                s_["mts"][ttile] = d.t(w, lambda e, p=p, ttile=ttile, off=off,
                                       pb=pb: e.matmul(
                    d.P[pb][:, off:off + 130],
                    AR[0:65, CKA + 128 * ttile:CKA + 128 * ttile + 128],
                    d.wvu_bf[:, 130 * p:130 * p + 130],
                    start=True, stop=True), inc=True)
                if sl in (2, 5) or ttile == 15:
                    n = (sl % 3) + 1
                    g0 = ttile - (sl % 3)
                    boff = offs[sl - (sl % 3)]
                    ev = d.v({"T": s_["mts"][ttile]},
                             lambda e, vaoff=vaoff, g0=g0,
                             n=n, boff=boff, pb=pb: e.tensor_copy(
                        AR[:, vaoff + 130 * g0:vaoff + 130 * (g0 + n)],
                        d.P[pb][:, boff:boff + 130 * n]))
                    s_["evac"][2 * cyc + bank] = ev
                    s_["last"] = ev
            if thi == 16:
                d.pread([(pb, 0)], "V", s_["evac"][4])
                d.pread([(pb, 1)], "V", s_["last"])
            return s_["last"]

        vg_done = {0: vgen_part(0, 0, 16, {"T": st.get("attnv_6", 0)})}
        exp_S = {}
        attnv_T = {}

        def emit_scores(g):
            p, j = g // 16, g % 16
            pn = "P01" if g % 2 == 0 else "P23"
            w = d.pwait([(pn, 0), (pn, 1)])
            if g < 4:
                w = d.merge(w, {"V": max(qn_red[7], self.ck_V)})
            if j < 2:
                w = d.merge(w, {"V": qev[p]})
            d.t(w, lambda e, pn=pn, j=j, p=p: e.matmul(
                d.P[pn][:, 0:512],
                AR[0:64, CKV2 + 128 * j:CKV2 + 128 * j + 128],
                AR[0:64, QTB + 512 * p:QTB + 512 * p + 512],
                start=True, stop=True, tile_position=(0, 0)), inc=False)
            smT = d.t({}, lambda e, pn=pn, j=j, p=p: e.matmul(
                d.P[pn][:, 512:1024],
                AR[64:128, CKV2 + 128 * j:CKV2 + 128 * j + 128],
                AR[64:128, QTB + 512 * p:QTB + 512 * p + 512],
                start=True, stop=True, tile_position=(64, 0)), inc=True)
            we = {"T": max(smT * 0, smT), "G3": gcb}
            we["T"] = max(smT, attnv_T.get(g - 2, st.get("attnv_7", 0)))
            ss = d.s(we, lambda e, pn=pn, g=g, cbslot=cbslot: e.activation(
                AR[:, ETB + 1024 * (g % 2):ETB + 1024 * (g % 2) + 1024],
                d.P[pn][:, 0:1024], AF.Exp,
                bias=d.Cb[:, cbslot:cbslot + 1], scale=0.125))
            exp_S[g] = ss
            d.pread([(pn, 0), (pn, 1)], "S", ss)

        def emit_attnv(g):
            p, ja = g // 16, g % 16
            pb = "P45" if p % 2 == 0 else "P67"
            wa = {"S": exp_S[g]}
            if ja == 0:
                wa = d.merge(wa, {"V": vg_done[p]},
                             d.pwait([(pb, 0), (pb, 1)]))
            vaoff = VA + 2080 * (p % 2)
            d.t(wa, lambda e, ja=ja, vaoff=vaoff, pb=pb, g=g: e.matmul(
                d.P[pb][0:65, 0:512],
                AR[:, vaoff + 130 * ja:vaoff + 130 * ja + 65],
                AR[:, ETB + 1024 * (g % 2):ETB + 1024 * (g % 2) + 512],
                start=(ja == 0), stop=(ja == 15)), inc=False)
            at = d.t({}, lambda e, ja=ja, vaoff=vaoff, pb=pb, g=g: e.matmul(
                d.P[pb][0:65, 512:1024],
                AR[:, vaoff + 130 * ja + 65:vaoff + 130 * ja + 130],
                AR[:, ETB + 1024 * (g % 2) + 512:
                   ETB + 1024 * (g % 2) + 1024],
                start=(ja == 0), stop=(ja == 15)), inc=True)
            attnv_T[g] = at
            if ja == 15:
                emit_norm(p)

        nst = {}

        def norm_finish(pp):
            bcb = d.gsc if pp % 2 == 0 else d.scr
            self._rv_buf[pp % 2] = d.v(
                {"G0": nst[pp]}, lambda e, bcb=bcb: e.reciprocal(
                    d.xs[0:64, 0:1024], bcb[0:64, 0:1024]))
            d.v({}, lambda e, pp=pp: e.tensor_tensor(
                AR[0:64, YT1 + 512 * pp:YT1 + 512 * pp + 512],
                d.outst[0:64, 0:512], d.xs[0:64, 0:512], ALU.mult))
            y2 = d.v({}, lambda e, pp=pp: e.tensor_tensor(
                AR[64:128, YT1 + 512 * pp:YT1 + 512 * pp + 512],
                d.outst[0:64, 512:1024], d.xs[0:64, 512:1024],
                ALU.mult))
            self._ylast = y2

        def emit_norm(p):
            pb = "P45" if p % 2 == 0 else "P67"
            if p >= 1:
                norm_finish(p - 1)
            d.v({"T": attnv_T[16 * p + 15], "G0": self._dg1_prev},
                lambda e, pb=pb: e.tensor_copy(
                d.outst[0:65, 0:512], d.P[pb][0:65, 0:512]))
            cpv = d.v({}, lambda e, pb=pb: e.tensor_copy(
                d.outst[0:65, 512:1024], d.P[pb][0:65, 512:1024]))
            d.pread([(pb, 0), (pb, 1)], "V", cpv)
            dg1 = d.dg({"V": cpv, "G0": self._bcg_prev},
                       lambda e: e.dma_start(
                out=d.R3[0:1, 0:1024], in_=d.outst[64:65, 0:1024]))
            self._dg1_prev = dg1
            bcb = d.gsc if p % 2 == 0 else d.scr
            bcg = d.dg({"G0": dg1, "V": self._rv_buf.get(p % 2, 0)},
                       lambda e, bcb=bcb: e.dma_start(
                out=bcb[0:64, 0:1024],
                in_=bass.AP(d.R3, 0, [[1, 1], [0, 64], [1, 1024]])))
            self._bcg_prev = bcg
            nst[p] = bcg

        for g in range(130):
            if conv_left and g % 10 == 5:
                self.conv_piece(conv_left.pop(0))
            if g >= 2:
                emit_attnv(g - 2)
            if g < 128:
                emit_scores(g)
            if g % 16 in (8, 10, 12, 14) and g // 16 < 7:
                p = g // 16
                part = (g % 16 - 8) // 2
                vg_done[p + 1] = vgen_part(
                    p + 1, 4 * part, 4 * part + 4,
                    {"T": attnv_T.get(16 * (p - 1) + 15,
                                      st.get("attnv_7", 0))})
        assert not conv_left
        self._pend_proj = (b, blk, norm_finish)
        st["attnv_6"] = attnv_T[16 * 6 + 15]
        st["attnv_7"] = attnv_T[16 * 7 + 15]
        st["exp_m2"] = st.get("exp_m1", self.entS)
        st["exp_m1"] = exp_S[127]
        st["gcb_m2"] = st.get("gcb_m1", 0)
        st["gcb_m1"] = gcb

    def _emit_proj(self, b, blk, ylast):
        d = self
        AR = d.AR
        tq0 = 512 * blk
        slots = [("P01", 0), ("P01", 512), ("P23", 0), ("P23", 512)]
        self._ylast_final = ylast
        for cc in range(8):
            pn, off = slots[cc % 4]
            pt = d.P[pn]
            for k in range(8):
                w = {}
                if k == 0:
                    w = d.pwait([(pn, off // 512)])
                    if cc == 0:
                        w = d.merge(w, {"V": ylast})
                d.t(w, lambda e, pt=pt, off=off, k=k, cc=cc: e.matmul(
                    pt[:, off:off + 512],
                    AR[:, WPC + 1024 * k + 128 * cc:
                       WPC + 1024 * k + 128 * cc + 128],
                    AR[:, YT1 + 512 * k:YT1 + 512 * k + 512],
                    start=(k == 0), stop=(k == 7)), inc=(k == 7))
            tt = d.cnt["T"]
            ev = d.v({"T": tt}, lambda e, pt=pt, off=off, cc=cc, tq0=tq0:
                     e.tensor_scalar(
                d.sT[:, 2048 * cc + tq0:2048 * cc + tq0 + 512],
                pt[:, off:off + 512], d.bpc[:, cc:cc + 1], None, ALU.add))
            d.pread([(pn, off // 512)], "V", ev)

    def ffn(self, b):
        d = self
        if self._pend_proj is not None:
            pb_, pblk_, fin_ = self._pend_proj
            self._pend_proj = None
            fin_(7)
            self._emit_proj(pb_, pblk_, self._ylast)
        AR = d.AR
        xrow = 1024 * b
        ffnT, ffnS = self.cnt["T"], self.cnt["S"]
        hp_v = {}
        silu_S = {}
        cv_v = {}
        cvg_v = {}
        gu_byi = {}
        prev_half_T = 0
        for half in range(2):
            for hc in range(24):
                i = half * 24 + hc
                par = i % 2
                ga_pn = "P01" if par == 0 else "P45"
                gb_pn = "P23" if par == 0 else "P67"
                if i == 0:
                    wq_ = {"V": max(self.entV, self.cvp_c.get(47, 0))}
                elif hc == 0:
                    wq_ = {"V": cvg_v[i - 1] if False else cv_v[i - 1]}
                else:
                    wq_ = {"S": cvg_v[i - 1]}
                wq2 = dict(wq_) if i == 0 or hc == 0 else {"S": cv_v[i - 1]}
                qg = d.dq(wq_, lambda e, hc=hc: e.dma_start(
                    out=bass.AP(d.IT, WFI,
                                [[2048, 128], [128, 8], [1, 128]]),
                    in_=bass.AP(d.d_wg.tensor, 128 * hc,
                                [[3072, 128], [393216, 8], [1, 128]])),
                          q="Q5")
                qu = d.dq(wq2, lambda e, hc=hc: e.dma_start(
                    out=bass.AP(d.IT, WFI + 1024,
                                [[2048, 128], [128, 8], [1, 128]]),
                    in_=bass.AP(d.d_wu.tensor, 128 * hc,
                                [[3072, 128], [393216, 8], [1, 128]])),
                          q="Q7")
                wcv = {"Q5": qg, "T": gu_byi.get(i - 2, 0)}
                if i == 0:
                    wcv = d.merge(wcv, {"T": ffnT, "S": ffnS})
                cvg_v[i] = d.s(wcv, lambda e, par=par: e.copy(
                    AR[:, WFB + 2048 * par:WFB + 2048 * par + 1024],
                    d.IT[:, WFI:WFI + 1024]))
                cv_v[i] = d.s({"Q7": qu}, lambda e, par=par: e.copy(
                    AR[:, WFB + 2048 * par + 1024:WFB + 2048 * par + 2048],
                    d.IT[:, WFI + 1024:WFI + 2048]))
                guT = {}
                for m in range(2):
                    pn = ga_pn if m == 0 else gb_pn
                    boff = 1024 * m
                    for tb in range(2):
                        w = {}
                        if m == 0 and tb == 0:
                            w = d.merge({"S": cvg_v[i]},
                                        d.pwait([(ga_pn, 0), (ga_pn, 1),
                                                 (gb_pn, 0), (gb_pn, 1)]))
                        if m == 1 and tb == 0:
                            w = {"S": cv_v[i]}
                        for c in range(8):
                            d.t(w if c == 0 else {}, lambda e, pn=pn, tb=tb,
                                c=c, par=par, boff=boff, half=half: e.matmul(
                                d.P[pn][:, 512 * tb:512 * tb + 512],
                                AR[:, WFB + 2048 * par + boff + 128 * c:
                                   WFB + 2048 * par + boff + 128 * c + 128],
                                d.xn[:, 2048 * c + 1024 * half + 512 * tb:
                                     2048 * c + 1024 * half + 512 * tb + 512],
                                start=(c == 0), stop=(c == 7)),
                                inc=(c == 7 and tb == 1))
                    guT[m] = d.cnt["T"]
                gu_byi[i] = guT[1]
                wsl = {"T": guT[0]}
                if i >= 2:
                    wsl["V"] = hp_v[i - 2]
                ssl = d.s(wsl, lambda e, ga_pn=ga_pn, par=par: e.activation(
                    AR[:, APR + 1024 * par:APR + 1024 * par + 1024],
                    d.P[ga_pn][:, 0:1024], AF.Silu, scale=S127))
                silu_S[i] = ssl
                d.pread([(ga_pn, 0), (ga_pn, 1)], "S", ssl)
                hv = d.v({"S": ssl, "T": guT[1]},
                         lambda e, gb_pn=gb_pn, par=par, hc=hc: e.tensor_tensor(
                    AR[:, HP + 1024 * hc:HP + 1024 * hc + 1024],
                    AR[:, APR + 1024 * par:APR + 1024 * par + 1024],
                    d.P[gb_pn][:, 0:1024], ALU.mult))
                hp_v[i] = hv
                d.pread([(gb_pn, 0), (gb_pn, 1)], "V", hv)
            dslots = [("P01", 0), ("P01", 512), ("P23", 0), ("P23", 512)]
            o2_v = {}
            dma_g = {}
            down_byc = {}
            wd_q = {}
            hlast = hp_v[half * 24 + 23]
            slast = silu_S[half * 24 + 23]
            for cc in range(8):
                par = cc % 2
                wl = {"T": max(down_byc.get(cc - 2, 0),
                               prev_half_T if cc < 2 else 0)}
                if cc < 2:
                    wl = d.merge(wl, {"V": hlast, "S": slast},
                                 self.wdb_done if half == 0 else {})
                wd_q[cc] = d.dq(wl, lambda e, cc=cc, par=par: e.dma_start(
                    out=bass.AP(d.AR, WDB + 3072 * par,
                                [[ACOLS, 128], [128, 24], [1, 128]]),
                    in_=bass.AP(d.d_wdb.tensor, 128 * cc,
                                [[1024, 128], [131072, 24], [1, 128]])),
                               q="Q6" if par == 0 else "Q4")
                self._wd_cv_last = 0
                for tb in range(2):
                    s4 = (2 * cc + tb) % 4
                    pn, off = dslots[s4]
                    pt = d.P[pn]
                    w = d.merge({"Q6" if par == 0 else "Q4": wd_q[cc]},
                                d.pwait([(pn, off // 512)]))
                    for hcc in range(24):
                        d.t(w if hcc == 0 else {}, lambda e, pt=pt, off=off,
                            hcc=hcc, par=par, tb=tb: e.matmul(
                            pt[:, off:off + 512],
                            AR[:, WDB + 3072 * par + 128 * hcc:
                               WDB + 3072 * par + 128 * hcc + 128],
                            AR[:, HP + 1024 * hcc + 512 * tb:
                               HP + 1024 * hcc + 512 * tb + 512],
                            start=(hcc == 0), stop=(hcc == 23)),
                            inc=(hcc == 23))
                    tt = d.cnt["T"]
                    down_byc[cc] = tt
                    s2 = (2 * cc + tb) % 2
                    wgs = {"T": tt, "V": o2_v.get(2 * cc + tb - 2, 0)}
                    gss = d.s(wgs, lambda e, pt=pt, off=off, s2=s2:
                              e.activation(d.gsc[:, 512 * s2:512 * s2 + 512],
                                           pt[:, off:off + 512], AF.Silu,
                                           scale=S127SQ))
                    d.pread([(pn, off // 512)], "S", gss)
                    xslot = (2 * cc + tb) % 4
                    qn_ = "Q%d" % xslot
                    lq = d.dq({"V": max(o2_v.get(2 * cc + tb - 4, 0),
                                        self._ylast_final
                                        if 2 * cc + tb < 4 else 0)},
                              lambda e, cc=cc, half=half, tb=tb, xslot=xslot:
                              e.dma_start(
                        out=d.xs[:, 512 * xslot:512 * xslot + 512],
                        in_=d.d_x[xrow + 128 * cc:xrow + 128 * cc + 128,
                                  1024 * half + 512 * tb:
                                  1024 * half + 512 * tb + 512]), q=qn_)
                    d.v({"S": gss}, lambda e, s2=s2, cc=cc, half=half, tb=tb:
                        e.tensor_tensor(
                        d.outst[:, 0:512],
                        d.gsc[:, 512 * s2:512 * s2 + 512],
                        d.sT[:, 2048 * cc + 1024 * half + 512 * tb:
                             2048 * cc + 1024 * half + 512 * tb + 512],
                        ALU.mult))
                    gq = "G%d" % (1 + s2)
                    ov = d.v({qn_: lq, gq: dma_g.get(2 * cc + tb - 2, 0)},
                             lambda e, s2=s2, xslot=xslot: e.tensor_tensor(
                        d.outst[:, 512 + 512 * s2:512 + 512 * s2 + 512],
                        d.outst[:, 0:512],
                        d.xs[:, 512 * xslot:512 * xslot + 512], ALU.add))
                    o2_v[2 * cc + tb] = ov
                    dma_g[2 * cc + tb] = d.dg(
                        {"V": ov}, lambda e, cc=cc, half=half, tb=tb, s2=s2:
                        e.dma_start(
                            out=d.d_out[xrow + 128 * cc:xrow + 128 * cc + 128,
                                        1024 * half + 512 * tb:
                                        1024 * half + 512 * tb + 512],
                            in_=d.outst[:, 512 + 512 * s2:
                                        512 + 512 * s2 + 512]), q=gq)
            prev_half_T = d.cnt["T"]


def kernel(**inputs):
    f32 = np.float32
    x = np.asarray(inputs["L0_attn_x"], f32)
    wq = np.ascontiguousarray(np.asarray(inputs["L0_attn_wq_d"], f32))
    bq = np.ascontiguousarray(np.asarray(inputs["L0_attn_bq_d"], f32).reshape(512, 1))
    wqk = np.asarray(inputs["L0_attn_w_qk"], f32)
    bqk = np.ascontiguousarray(np.asarray(inputs["L0_attn_b_qk"], f32).reshape(1024, 1))
    wkv = np.asarray(inputs["L0_attn_wkv_d"], f32)
    bkv = np.ascontiguousarray(np.asarray(inputs["L0_attn_bkv_d"], f32).reshape(64, 1))
    wvu = np.asarray(inputs["L0_attn_wv_u"], f32)
    bvu = np.asarray(inputs["L0_attn_bv_u"], f32)
    wp = np.asarray(inputs["L0_attn_w_proj"], f32)
    bp = np.ascontiguousarray(np.asarray(inputs["L0_attn_b_proj"], f32).reshape(1024, 1))
    wg = np.asarray(inputs["L0_ffn_w_gate"], np.int32)
    wu = np.asarray(inputs["L0_ffn_w_up"], np.int32)
    wd = np.asarray(inputs["L0_ffn_w_down"], np.int32)

    wqkT = np.ascontiguousarray(wqk.T)
    wkvT = np.ascontiguousarray(wkv.T)
    wpT = np.ascontiguousarray(wp.T)
    wgT = np.ascontiguousarray(wg.T)
    wuT = np.ascontiguousarray(wu.T)
    wdT = np.ascontiguousarray(wd.T)
    wvuT = wvu.T
    wvu_aug = np.zeros((65, 1040), f32)
    for h in range(16):
        wvu_aug[0:64, 65 * h:65 * h + 64] = wvuT[:, 64 * h:64 * h + 64]
        wvu_aug[64, 65 * h:65 * h + 64] = bvu[64 * h:64 * h + 64]
        wvu_aug[64, 65 * h + 64] = 1.0
    wvu_aug = np.ascontiguousarray(wvu_aug)

    nc = KB().build()
    in_maps = []
    for c in range(8):
        xsh = np.ascontiguousarray(
            x[2 * c:2 * c + 2].transpose(0, 2, 1)).reshape(2048, 2048)
        in_maps.append({
            "xT": xsh, "wq": wq, "wqkT": wqkT, "bq": bq, "bqk": bqk,
            "wkvT": wkvT, "bkv": bkv, "wvu": wvu_aug, "wpT": wpT, "bp": bp,
            "wgT": wgT, "wuT": wuT, "wdT": wdT,
        })
    res = run_bass_kernel_spmd(nc, in_maps, core_ids=list(range(8)))
    out = np.empty((16, 2048, 1024), f32)
    for c in range(8):
        o = np.asarray(res.results[c]["outT"]).reshape(2, 1024, 2048)
        out[2 * c:2 * c + 2] = o.transpose(0, 2, 1)
    return out


if __name__ == "__main__":
    KB().build()
    print("build ok")


# version 46
# speedup vs baseline: 1.2474x; 1.0077x over previous
import contextlib
import numpy as np

import concourse.bass as bass
import concourse.mybir as mybir
from concourse.bass_utils import run_bass_kernel_spmd

F32 = mybir.dt.float32
BF16 = mybir.dt.bfloat16
I32 = mybir.dt.int32
AF = mybir.ActivationFunctionType
ALU = mybir.AluOpType
AX = mybir.AxisListType

T = 2048
C = 1024
H3 = 3072
S127 = float(np.float32(1.0 / 127.0))
S127SQ = float(np.float32(1.0 / 127.0)) ** 2

WC = 0
CKA = 8192
CKV2 = 10240
VA = 12288
QTB = 16448
ETB = 20544
YT1 = 22592
WPC = 26688
ACOLS = 34880
ABC = 12288
BBC = 14336
WQB = 16448
WQKB = 20544
HP = 0
WFB = 24576
APR = 28672
WDB = 28672

WFI = 0
WDI = 0


class KB:
    def __init__(self):
        self.nc = bass.Bass("TRN2", target_bir_lowering=False, debug=False)
        self.cnt = {"T": 0, "V": 0, "S": 0}
        for i in range(8):
            self.cnt["Q%d" % i] = 0
        for i in range(4):
            self.cnt["G%d" % i] = 0
        self.q = {"tensor": [], "vector": [], "scalar": [], "sync": [],
                  "gpsimd": []}
        self.pr = {}

    def flush(self):
        for eng in list(self.q):
            lst = self.q[eng]
            if lst:
                def body(h, lst=lst):
                    for f in lst:
                        f(h)
                getattr(self.block, eng)(body)
            self.q[eng] = []

    def _w(self, e, waits):
        for k, v in waits.items():
            if v and v > 0:
                e.wait_ge(self.sem[k],
                          v * 16 if k[0] in ("Q", "G") else v)

    def _mk(self, eng, waits, fn, inc_key=None):
        w = dict(waits)
        sem = self.sem[inc_key] if inc_key else None
        amt = 16 if (inc_key and inc_key[0] in ("Q", "G")) else 1
        def g(e, w=w, fn=fn, sem=sem, amt=amt):
            self._w(e, w)
            r = fn(e)
            if sem is not None:
                r.then_inc(sem, amt)
        self.q[eng].append(g)
        if inc_key:
            self.cnt[inc_key] += 1
            return self.cnt[inc_key]
        return None

    def t(self, waits, fn, inc=False):
        r = self._mk("tensor", waits, fn, "T" if inc else None)
        return r if r is not None else self.cnt["T"]

    def v(self, waits, fn):
        return self._mk("vector", waits, fn, "V")

    def s(self, waits, fn):
        return self._mk("scalar", waits, fn, "S")

    def dq(self, waits, fn, q="Q7"):
        return self._mk("sync", waits, fn, q)

    def dg(self, waits, fn, q="G0"):
        return self._mk("gpsimd", waits, fn, q)

    def pread(self, keys, eng, val):
        for key in keys:
            d = self.pr.setdefault(key, {})
            d[eng] = max(d.get(eng, 0), val)

    def pwait(self, keys):
        out = {}
        for key in keys:
            for eng, v in self.pr.get(key, {}).items():
                out[eng] = max(out.get(eng, 0), v)
        return out

    @staticmethod
    def merge(*ds):
        out = {}
        for d in ds:
            for k, v in d.items():
                out[k] = max(out.get(k, 0), v)
        return out

    def bc_src(self, tens, cols):
        return bass.AP(tens, 0, [[1, 1], [0, 128], [1, cols]])

    def build(self):
        nc = self.nc
        self.d_x = nc.dram_tensor("xT", [2 * C, T], F32, kind="ExternalInput").ap()
        self.d_wq = nc.dram_tensor("wq", [512, C], F32, kind="ExternalInput").ap()
        self.d_wqkT = nc.dram_tensor("wqkT", [512, C], F32, kind="ExternalInput").ap()
        self.d_bq = nc.dram_tensor("bq", [512, 1], F32, kind="ExternalInput").ap()
        self.d_bqk = nc.dram_tensor("bqk", [C, 1], F32, kind="ExternalInput").ap()
        self.d_wkvT = nc.dram_tensor("wkvT", [C, 64], F32, kind="ExternalInput").ap()
        self.d_bkv = nc.dram_tensor("bkv", [64, 1], F32, kind="ExternalInput").ap()
        self.d_wvu = nc.dram_tensor("wvu", [65, 1040], F32, kind="ExternalInput").ap()
        self.d_wpT = nc.dram_tensor("wpT", [C, C], F32, kind="ExternalInput").ap()
        self.d_bp = nc.dram_tensor("bp", [C, 1], F32, kind="ExternalInput").ap()
        self.d_wg = nc.dram_tensor("wgT", [C, H3], I32, kind="ExternalInput").ap()
        self.d_wu = nc.dram_tensor("wuT", [C, H3], I32, kind="ExternalInput").ap()
        self.d_wd = nc.dram_tensor("wdT", [H3, C], I32, kind="ExternalInput").ap()
        self.d_out = nc.dram_tensor("outT", [2 * C, T], F32,
                                    kind="ExternalOutput").ap()
        self.d_wdb = nc.dram_tensor("wdB", [H3, C], BF16, kind="Internal").ap()

        with contextlib.ExitStack() as es:
            sb = lambda n, s, d: es.enter_context(nc.sbuf_tensor(n, s, d))
            self.AR = sb("AR", [128, ACOLS], BF16)
            self.IT = sb("IT", [128, 2048], I32)
            self.xn = sb("xn", [128, 16384], BF16)
            self.sT = sb("sTb", [128, 16384], BF16)
            self.wkv_bf = sb("wkv_bf", [128, 512], BF16)
            self.wvu_bf = sb("wvu_bf", [65, 1040], BF16)
            self.xs = sb("xs", [128, 2048], F32)
            self.CONVST = sb("CONVST", [128, 1024], BF16)
            self.scr = sb("scr", [128, 1024], F32)
            self.outst = sb("outst", [128, 1536], F32)
            self.gsc = sb("gsc", [128, 1024], F32)
            self.R1 = sb("R1", [1, 2048], F32)
            self.R2 = sb("R2", [1, 2048], F32)
            self.R3 = sb("R3", [1, 2048], F32)
            self.R4 = sb("R4", [1, 2048], F32)
            self.arow_bf = sb("arow_bf", [1, 2048], BF16)
            self.brow_bf = sb("brow_bf", [1, 2048], BF16)
            self.bqc = sb("bqc", [128, 4], F32)
            self.bqc_bf = sb("bqc_bf", [128, 4], BF16)
            self.bqkc = sb("bqkc", [128, 8], F32)
            self.bkv_sb = sb("bkv_sb", [64, 1], F32)
            self.bpc = sb("bpc", [128, 8], F32)
            self.bcomb = sb("bcomb", [128, 8], F32)
            self.ones_f = sb("ones_f", [128, 1], F32)
            self.onesr = sb("onesr", [1, 64], F32)
            self.qnrow = sb("qnrow", [1, 16], F32)
            self.mrow = sb("mrow", [1, 16], F32)
            self.cn = [sb("cn0", [1, 1], F32), sb("cn1", [1, 1], F32)]
            self.Cb = sb("Cb", [128, 2], F32)
            self.P = {}
            for nm in ("P01", "P23", "P45", "P67"):
                self.P[nm] = es.enter_context(nc.psum_tensor(nm, [128, 1024], F32))
            self.sem = {}
            for k in (["T", "V", "S"] + ["Q%d" % i for i in range(8)]
                      + ["G%d" % i for i in range(4)]):
                self.sem[k] = es.enter_context(nc.semaphore("SEM" + k))
            self.block = es.enter_context(nc.Block())

            self.init_phase()
            self.flush()
            self.entT, self.entV, self.entS = self.cnt["T"], self.cnt["V"], 0
            self.entG1, self.entG2 = self.cnt["G1"], self.cnt["G2"]
            self.entG0 = self.cnt["G0"]
            for b in range(2):
                self.fold(b)
                self.flush()
                self.ln(b)
                self.flush()
                self.qkv(b)
                self.flush()
                self.st = {"exp_m1": self.entS}
                for blk in range(4):
                    self.attn_blk(b, blk)
                    self.flush()
                self.ffn(b)
                self.flush()
                self.entT, self.entV, self.entS = (self.cnt["T"], self.cnt["V"],
                                                   self.cnt["S"])
                self.entG0 = self.cnt["G0"]
                self.entG1, self.entG2 = self.cnt["G1"], self.cnt["G2"]
            tot = {k: self.cnt[k] for k in self.cnt if k[0] in ("Q", "G")}
            def fin(e, tot=tot):
                for k, v in tot.items():
                    if v:
                        e.wait_ge(self.sem[k], 16 * v)
            self.q["gpsimd"].append(fin)
            self.flush()
        return nc

    def init_phase(self):
        d = self
        for k in range(4):
            d.dq({}, lambda e, k=k: e.dma_start(
                out=d.bqc[:, k:k + 1], in_=d.d_bq[128 * k:128 * k + 128, :]))
        for k in range(8):
            d.dq({}, lambda e, k=k: e.dma_start(
                out=d.bqkc[:, k:k + 1], in_=d.d_bqk[128 * k:128 * k + 128, :]))
        d.dq({}, lambda e: e.dma_start(out=d.bkv_sb[:, :], in_=d.d_bkv))
        for k in range(8):
            d.dq({}, lambda e, k=k: e.dma_start(
                out=d.bpc[:, k:k + 1], in_=d.d_bp[128 * k:128 * k + 128, :]))
        for k in range(8):
            d.dq({}, lambda e, k=k: e.dma_start(
                out=d.scr[:, 64 * k:64 * k + 64],
                in_=d.d_wkvT[128 * k:128 * k + 128, :]))
        q1 = d.dq({}, lambda e: e.dma_start(out=d.xs[0:65, 0:1040], in_=d.d_wvu))
        d.v({"Q7": q1}, lambda e: e.tensor_copy(d.wkv_bf[:, :], d.scr[:, 0:512]))
        d.v({}, lambda e: e.tensor_copy(d.bqc_bf[:, :], d.bqc[:, :]))
        d.v({}, lambda e: e.tensor_copy(d.wvu_bf[:, :], d.xs[0:65, 0:1040]))
        d.v({}, lambda e: e.memset(d.ones_f[:, :], 1.0))
        d.v({}, lambda e: e.memset(d.onesr[:, :], 1.0))
        self.cvp_c = {}
        self.stp_c = {}
        self._bcg_prev = 0
        self._pend_proj = None
        self._rv_buf = {}
        self._dg1_prev = 0
        self._ylast_final = 0
        self.wdb_done = {}
        self.v_init = d.cnt["V"]
        self.entV = self.v_init

    def conv_piece(self, pc):
        d = self
        r0 = 128 * (pc // 2)
        c0 = 512 * (pc % 2)
        par = pc % 4
        sl = pc % 2
        qn_ = "Q5" if sl == 0 else "Q7"
        gq = "G1" if sl == 0 else "G2"
        lq = d.dq({"V": d.cvp_c.get(pc - 4, 0)},
                  lambda e, r0=r0, c0=c0, par=par:
                  e.dma_start(out=d.IT[:, 512 * par:512 * par + 512],
                              in_=d.d_wd[r0:r0 + 128, c0:c0 + 512]), q=qn_)
        wc = {qn_: lq, gq: d.stp_c.get(pc - 2, 0)}
        d.cvp_c[pc] = d.v(wc, lambda e, par=par, sl=sl: e.tensor_copy(
            d.CONVST[:, 512 * sl:512 * sl + 512],
            d.IT[:, 512 * par:512 * par + 512]))
        d.stp_c[pc] = d.dg({"V": d.cvp_c[pc]},
                           lambda e, r0=r0, c0=c0, sl=sl:
                           e.dma_start(out=d.d_wdb[r0:r0 + 128, c0:c0 + 512],
                                       in_=d.CONVST[:, 512 * sl:512 * sl + 512]),
                           q=gq)
        if pc == 47:
            self.wdb_done = {"G1": d.stp_c[46], "G2": d.stp_c[47]}

    def fold(self, b):
        d = self
        AR = d.AR
        entT, entV, entS = self.entT, self.entV, self.entS
        cv = []
        for n in range(8):
            src = d.d_wq if n < 4 else d.d_wqkT
            k = n % 4
            dst_off = (WQB if n < 4 else WQKB) + 1024 * k
            stage = d.scr[:, :] if n % 2 == 0 else d.xs[:, 0:1024]
            qn_ = "Q4" if n % 2 == 0 else "Q2"
            wv = {"V": cv[n - 2] if n >= 2 else entV}
            qq = d.dq(wv, lambda e, src=src, k=k, stage=stage: e.dma_start(
                out=stage, in_=src[128 * k:128 * k + 128, :]), q=qn_)
            wv2 = {qn_: qq}
            if n == 0:
                wv2["T"] = entT
                wv2["S"] = entS
            cv.append(d.v(wv2, lambda e, dst_off=dst_off, stage=stage:
                          e.tensor_copy(AR[:, dst_off:dst_off + 1024],
                                        stage)))
        conv_done = cv[-1]
        wq_done = cv[3]

        slots = [("P01", 0), ("P01", 512), ("P23", 0), ("P23", 512)]
        wc_ev = []
        n = 0
        for i in range(8):
            for j2 in range(2):
                pn, off = slots[n % 4]
                pt = d.P[pn]
                for ql in range(4):
                    w = {}
                    if ql == 0:
                        w["V"] = wc_ev[n - 4] if n >= 4 else conv_done
                        w = d.merge(w, d.pwait([(pn, off // 512)]))
                    d.t(w, lambda e, pt=pt, off=off, ql=ql, i=i, j2=j2: e.matmul(
                        pt[:, off:off + 512],
                        AR[:, WQB + 1024 * ql + 128 * i:
                           WQB + 1024 * ql + 128 * i + 128],
                        AR[:, WQKB + 1024 * ql + 512 * j2:
                           WQKB + 1024 * ql + 512 * j2 + 512],
                        start=(ql == 0), stop=(ql == 3)), inc=(ql == 3))
                tt = d.cnt["T"]
                wev = {"T": tt}
                ev = d.v(wev, lambda e, pt=pt, off=off, i=i, j2=j2:
                         e.tensor_copy(AR[:, WC + 1024 * i + 512 * j2:
                                          WC + 1024 * i + 512 * j2 + 512],
                                       pt[:, off:off + 512]))
                d.pread([(pn, off // 512)], "V", ev)
                wc_ev.append(ev)
                n += 1
        for j in range(8):
            off = 512 * (j % 2)
            for ql in range(4):
                w = {}
                if ql == 0:
                    w = d.pwait([("P67", off // 512)])
                    if j == 0:
                        w = d.merge(w, {"V": wq_done})
                d.t(w, lambda e, j=j, ql=ql, off=off: e.matmul(
                    d.P["P67"][:, off:off + 1],
                    AR[:, WQKB + 1024 * ql + 128 * j:
                       WQKB + 1024 * ql + 128 * j + 128],
                    d.bqc_bf[:, ql:ql + 1], start=(ql == 0), stop=(ql == 3)),
                    inc=(ql == 3))
            tt = d.cnt["T"]
            ev = d.v({"T": tt}, lambda e, j=j, off=off: e.tensor_tensor(
                d.bcomb[:, j:j + 1], d.P["P67"][:, off:off + 1],
                d.bqkc[:, j:j + 1], ALU.add))
            d.pread([("P67", off // 512)], "V", ev)
        self.fold_T = d.cnt["T"]
        self.fold_V = d.cnt["V"]

    def ln(self, b):
        d = self
        AR = d.AR
        xrow = 1024 * b
        ld = {}
        mm = {}
        sqv = {}

        def lslot(i):
            return (d.outst, 0) if i % 2 == 0 else (d.xs, 1024)

        def load(i):
            half, k = i // 8, i % 8
            buf, off = lslot(i)
            wv = {"T": mm.get(i - 2, 0)}
            if i < 2:
                if b == 0:
                    wv = d.merge(wv, {"V": self.v_init})
                else:
                    wv = d.merge(wv, {"V": self.entV, "G1": self.entG1,
                                      "G2": self.entG2})
            qn_ = ("Q0", "Q1", "Q3", "Q5")[i % 4]
            ld[i] = d.dq(wv, lambda e, half=half, k=k, buf=buf, off=off:
                         e.dma_start(
                out=buf[:, off:off + 1024],
                in_=d.d_x[xrow + 128 * k:xrow + 128 * k + 128,
                          1024 * half:1024 * half + 1024]), q=qn_)

        def comp(i):
            half, k = i // 8, i % 8
            buf, off = lslot(i)
            qn_ = ("Q0", "Q1", "Q3", "Q5")[i % 4]
            wsq = {qn_: ld[i], "T": mm.get(i - 1, 0)}
            if b == 1 and i == 0:
                wsq["S"] = self.entS
            sqv[i] = d.v(wsq, lambda e, buf=buf, off=off: e.tensor_tensor(
                d.gsc[:, 0:1024], buf[:, off:off + 1024],
                buf[:, off:off + 1024], ALU.mult))
            w0 = {qn_: ld[i]}
            if k == 0:
                w0 = d.merge(w0, d.pwait([("P45", 0), ("P45", 1)]))

            for nb in range(2):
                d.t(w0 if nb == 0 else {}, lambda e, buf=buf, off=off, nb=nb,
                    k=k: e.matmul(d.P["P45"][0:1, 512 * nb:512 * nb + 512],
                                  d.ones_f[:, :],
                                  buf[:, off + 512 * nb:off + 512 * nb + 512],
                                  start=(k == 0), stop=(k == 7)), inc=False)
            wsq2 = {"V": sqv[i]}
            if k == 0:
                wsq2 = d.merge(wsq2, d.pwait([("P67", 0), ("P67", 1)]))
            for nb in range(2):
                d.t(wsq2 if nb == 0 else {}, lambda e, nb=nb, k=k:
                    e.matmul(d.P["P67"][0:1, 512 * nb:512 * nb + 512],
                             d.ones_f[:, :],
                             d.gsc[:, 512 * nb:512 * nb + 512],
                             start=(k == 0), stop=(k == 7)),
                    inc=(nb == 1))
            mm[i] = d.cnt["T"]
            if k == 7:
                tt = d.cnt["T"]
                for nb in range(2):
                    ev = d.v({"T": tt}, lambda e, half=half, nb=nb:
                             e.tensor_copy(
                        d.R1[0:1, 1024 * half + 512 * nb:
                             1024 * half + 512 * nb + 512],
                        d.P["P45"][0:1, 512 * nb:512 * nb + 512]))
                    d.pread([("P45", nb)], "V", ev)
                    ev = d.v({}, lambda e, half=half, nb=nb: e.tensor_copy(
                        d.R2[0:1, 1024 * half + 512 * nb:
                             1024 * half + 512 * nb + 512],
                        d.P["P67"][0:1, 512 * nb:512 * nb + 512]))
                    d.pread([("P67", nb)], "V", ev)

        for i in range(16):
            load(i)
            if i >= 1:
                comp(i - 1)
        comp(15)
        d.v({"T": self.entT, "G0": self.entG0}, lambda e: e.tensor_scalar_mul(
            d.R3[:, :], d.R1[:, :], 1.0 / 1024.0))
        d.v({}, lambda e: e.tensor_scalar_mul(d.R4[:, :], d.R2[:, :],
                                              1.0 / 1024.0))
        d.v({}, lambda e: e.tensor_tensor(d.R1[:, :], d.R3[:, :],
                                          d.R3[:, :], ALU.mult))
        d.v({}, lambda e: e.tensor_tensor(d.R2[:, :], d.R4[:, :],
                                          d.R1[:, :], ALU.subtract))
        vv = d.v({}, lambda e: e.tensor_scalar_add(d.R4[:, :], d.R2[:, :],
                                                   1e-5))
        ss = d.s({"V": vv}, lambda e: e.activation(d.R2[:, :], d.R4[:, :],
                                                   AF.Sqrt))
        d.v({"S": ss}, lambda e: e.reciprocal(d.R4[:, :], d.R2[:, :]))
        d.v({}, lambda e: e.scalar_tensor_tensor(
            d.R1[:, :], d.R3[:, :], -1.0, d.R4[:, :], ALU.mult, ALU.mult))
        d.v({}, lambda e: e.tensor_copy(d.arow_bf[:, :], d.R4[:, :]))
        vv = d.v({}, lambda e: e.tensor_copy(d.brow_bf[:, :], d.R1[:, :]))
        g1 = d.dg({"V": vv, "T": self.entT, "S": self.entS},
                  lambda e: e.dma_start(out=AR[:, ABC:ABC + 2048],
                                        in_=d.bc_src(d.arow_bf, 2048)))
        g2 = d.dg({}, lambda e: e.dma_start(out=AR[:, BBC:BBC + 2048],
                                            in_=d.bc_src(d.brow_bf, 2048)))
        xv = {}
        mulv = {}
        for i in range(16):
            half, k = i // 8, i % 8
            buf, off = lslot(i)
            if i < 2:
                wv = {"T": mm[14 + (i % 2)], "V": vv}
            else:
                wv = {"V": mulv[i - 2]}
            qn_ = ("Q0", "Q1", "Q3", "Q5")[i % 4]
            lda = d.dq(wv, lambda e, k=k, half=half, buf=buf, off=off:
                       e.dma_start(
                out=buf[:, off:off + 1024],
                in_=d.d_x[xrow + 128 * k:xrow + 128 * k + 128,
                          1024 * half:1024 * half + 1024]), q=qn_)
            w1 = {qn_: lda, "G0": g2}
            if i == 0:
                w1["T"] = self.entT
            mulv[i] = d.v(w1, lambda e, buf=buf, off=off, half=half:
                          e.tensor_tensor(
                d.gsc[:, 0:1024], buf[:, off:off + 1024],
                AR[:, ABC + 1024 * half:ABC + 1024 * half + 1024], ALU.mult))
            xv[i] = d.v({}, lambda e, half=half, k=k: e.tensor_tensor(
                d.xn[:, 2048 * k + 1024 * half:
                     2048 * k + 1024 * half + 1024],
                d.gsc[:, 0:1024],
                AR[:, BBC + 1024 * half:BBC + 1024 * half + 1024], ALU.add))
            if i == 7:
                self.ln_h0_V = xv[7]
        self.ln_V = d.cnt["V"]

    def qkv(self, b):
        d = self
        AR = d.AR
        m1 = d.v({"T": self.entT, "S": self.entS},
                 lambda e: e.memset(AR[64:65, CKA:CKA + 2048], 1.0))
        ck_ev = [0] * 4
        for nb in range(4):
            off = 512 * (nb % 2)
            for c in range(8):
                w = {}
                if c == 0:
                    w = d.pwait([("P45", nb % 2)])
                    w = d.merge(w, {"V": self.ln_h0_V if nb < 2 else
                                    max(self.ln_V, ck_ev[nb - 2])})
                d.t(w, lambda e, off=off, c=c, nb=nb: e.matmul(
                    d.P["P45"][0:64, off:off + 512],
                    d.wkv_bf[:, 64 * c:64 * c + 64],
                    d.xn[:, 2048 * c + 512 * nb:2048 * c + 512 * nb + 512],
                    start=(c == 0), stop=(c == 7)), inc=(c == 7))
            tt = d.cnt["T"]
            ev1 = d.v({"T": tt, "V": m1}, lambda e, off=off, nb=nb:
                      e.tensor_scalar(
                          AR[0:64, CKA + 512 * nb:CKA + 512 * nb + 512],
                          d.P["P45"][0:64, off:off + 512], d.bkv_sb[:, :], None,
                          ALU.add))
            d.pread([("P45", nb % 2)], "V", ev1)
            d.v({}, lambda e, nb=nb: e.tensor_copy(
                AR[0:64, CKV2 + 512 * nb:CKV2 + 512 * nb + 512],
                AR[0:64, CKA + 512 * nb:CKA + 512 * nb + 512]))
            ck_ev[nb] = d.v({}, lambda e, nb=nb: e.tensor_copy(
                AR[64:128, CKV2 + 512 * nb:CKV2 + 512 * nb + 512],
                AR[0:64, CKA + 512 * nb:CKA + 512 * nb + 512]))
        k2mmT = [0] * 4
        for nb in range(4):
            off = 512 * (nb % 2)
            wsq = {}
            if nb >= 1:
                wsq["T"] = k2mmT[nb - 1]
            sqv = d.v(wsq, lambda e, nb=nb: e.tensor_tensor(
                d.gsc[0:64, 0:512],
                AR[0:64, CKV2 + 512 * nb:CKV2 + 512 * nb + 512],
                AR[0:64, CKV2 + 512 * nb:CKV2 + 512 * nb + 512], ALU.mult))
            w = d.merge({"V": sqv}, d.pwait([("P67", nb % 2)]))
            k2mmT[nb] = d.t(w, lambda e, off=off: e.matmul(
                d.P["P67"][0:1, off:off + 512], d.ones_f[0:64, :],
                d.gsc[0:64, 0:512], start=True, stop=True), inc=True)
            ev = d.v({"T": k2mmT[nb]}, lambda e, off=off, nb=nb: e.tensor_copy(
                d.R4[0:1, 512 * nb:512 * nb + 512],
                d.P["P67"][0:1, off:off + 512]))
            d.pread([("P67", nb % 2)], "V", ev)
        d.v({}, lambda e: e.tensor_reduce(d.mrow[0:1, 0:1], d.R4[:, :], AX.X,
                                          ALU.max))
        self.kmax_V = d.cnt["V"]
        wp_cv = []
        for k in range(8):
            stage = d.scr[:, :] if k % 2 == 0 else d.xs[:, 0:1024]
            qn_ = "Q4" if k % 2 == 0 else "Q2"
            wv = {"V": wp_cv[k - 2]} if k >= 2 else {"V": self.fold_V}
            qq = d.dq(wv, lambda e, k=k, stage=stage: e.dma_start(
                out=stage, in_=d.d_wpT[128 * k:128 * k + 128, :]), q=qn_)
            w2 = {qn_: qq}
            if k == 0:
                w2["T"] = self.entT
                w2["S"] = self.entS
            wp_cv.append(d.v(w2, lambda e, k=k, stage=stage: e.tensor_copy(
                AR[:, WPC + 1024 * k:WPC + 1024 * k + 1024], stage)))
        self.wp_V = d.cnt["V"]
        self.ck_V = ck_ev[3]

    def attn_blk(self, b, blk):
        d = self
        AR = d.AR
        st = self.st
        tq0 = 512 * blk
        cbslot = blk % 2
        conv_left = list(range(12 * blk, 12 * blk + 12)) if b == 0 else []
        qev = [0] * 8
        qn_red = [0] * 8
        qn_mmT = [0] * 8
        sqv_l = [0] * 8

        def emit_qev(k):
            off = 512 * (k % 2)
            if k < 2:
                w = d.merge(d.pwait([("P01", k % 2)]),
                            {"V": max(self.wp_V,
                                      self.ln_h0_V if blk < 2 else self.ln_V)})
            else:
                w = {"V": qev[k - 2]}
            for c in range(8):
                d.t(w if c == 0 else {}, lambda e, off=off, c=c, k=k, tq0=tq0:
                    e.matmul(
                    d.P["P01"][:, off:off + 512],
                    AR[:, WC + 1024 * c + 128 * k:WC + 1024 * c + 128 * k + 128],
                    d.xn[:, 2048 * c + tq0:2048 * c + tq0 + 512],
                    start=(c == 0), stop=(c == 7)), inc=(c == 7))
            tt = d.cnt["T"]
            qev[k] = d.v({"T": tt}, lambda e, off=off, k=k: e.tensor_scalar(
                AR[:, QTB + 512 * k:QTB + 512 * k + 512],
                d.P["P01"][:, off:off + 512], d.bcomb[:, k:k + 1], None,
                ALU.add))
            d.pread([("P01", k % 2)], "V", qev[k])
            sbuf, soff = self._qnslot(k)
            wv = {"T": qn_mmT[k - 4]} if k >= 4 else {}
            sqv_l[k] = d.v(wv, lambda e, k=k, sbuf=sbuf, soff=soff:
                           e.tensor_tensor(
                sbuf[:, soff:soff + 512],
                AR[:, QTB + 512 * k:QTB + 512 * k + 512],
                AR[:, QTB + 512 * k:QTB + 512 * k + 512], ALU.mult))

        def emit_qn(k):
            off = 512 * (k % 2)
            sbuf, soff = self._qnslot(k)
            w2 = d.merge({"V": sqv_l[k]}, d.pwait([("P23", k % 2)]))
            qn_mmT[k] = d.t(w2, lambda e, sbuf=sbuf, soff=soff, off=off:
                            e.matmul(
                d.P["P23"][0:1, off:off + 512], d.ones_f[:, :],
                sbuf[:, soff:soff + 512], start=True, stop=True), inc=True)
            qn_red[k] = d.v({"T": qn_mmT[k]}, lambda e, k=k, off=off:
                            e.tensor_reduce(d.qnrow[0:1, k:k + 1],
                                            d.P["P23"][0:1, off:off + 512],
                                            AX.X, ALU.max))
            d.pread([("P23", k % 2)], "V", qn_red[k])

        for k in range(8):
            emit_qev(k)
            if k >= 2:
                emit_qn(k - 2)
        if self._pend_proj is not None:
            pb_, pblk_, fin_ = self._pend_proj
            self._pend_proj = None
            fin_(7)
            self._emit_proj(pb_, pblk_, self._ylast)
        emit_qn(6)
        emit_qn(7)
        d.v({"V": self.kmax_V}, lambda e: e.tensor_reduce(
            d.mrow[0:1, 1:2], d.qnrow[0:1, 0:8], AX.X, ALU.max))
        d.v({}, lambda e: e.tensor_tensor(d.mrow[0:1, 2:3], d.mrow[0:1, 1:2],
                                          d.mrow[0:1, 0:1], ALU.add))
        cvv = d.v({"G3": st.get("gcb_m2", 0)}, lambda e, cbslot=cbslot:
                  e.tensor_single_scalar(d.cn[cbslot][:, :], d.mrow[0:1, 2:3],
                                         -1.0 / 16.0, ALU.mult))
        wcb = {"V": cvv, "S": st.get("exp_m2", self.entS)}
        gcb = d.dg(wcb, lambda e, cbslot=cbslot: e.dma_start(
            out=d.Cb[:, cbslot:cbslot + 1],
            in_=bass.AP(d.cn[cbslot], 0, [[1, 1], [0, 128], [1, 1]])), q="G3")

        vst = {}

        def vgen_part(p, tlo, thi, extra_w):
            vaoff = VA + 2080 * (p % 2)
            pb = "P45" if p % 2 == 0 else "P67"
            offs = [0, 130, 260, 512, 642, 772]
            s_ = vst.setdefault(p, {"mts": {}, "evac": {}, "last": 0})
            for ttile in range(tlo, thi):
                sl = ttile % 6
                off = offs[sl]
                bank = sl // 3
                cyc = ttile // 6
                w = {}
                if sl in (0, 3):
                    if cyc == 0:
                        w = d.pwait([(pb, bank)])
                    else:
                        w = {"V": s_["evac"][2 * (cyc - 1) + bank]}
                if ttile == 0:
                    w = d.merge(w, {"V": self.ck_V}, extra_w)
                s_["mts"][ttile] = d.t(w, lambda e, p=p, ttile=ttile, off=off,
                                       pb=pb: e.matmul(
                    d.P[pb][:, off:off + 130],
                    AR[0:65, CKA + 128 * ttile:CKA + 128 * ttile + 128],
                    d.wvu_bf[:, 130 * p:130 * p + 130],
                    start=True, stop=True), inc=True)
                if sl in (2, 5) or ttile == 15:
                    n = (sl % 3) + 1
                    g0 = ttile - (sl % 3)
                    boff = offs[sl - (sl % 3)]
                    ev = d.v({"T": s_["mts"][ttile]},
                             lambda e, vaoff=vaoff, g0=g0,
                             n=n, boff=boff, pb=pb: e.tensor_copy(
                        AR[:, vaoff + 130 * g0:vaoff + 130 * (g0 + n)],
                        d.P[pb][:, boff:boff + 130 * n]))
                    s_["evac"][2 * cyc + bank] = ev
                    s_["last"] = ev
            if thi == 16:
                d.pread([(pb, 0)], "V", s_["evac"][4])
                d.pread([(pb, 1)], "V", s_["last"])
            return s_["last"]

        vg_done = {0: vgen_part(0, 0, 16, {"T": st.get("attnv_6", 0)})}
        exp_S = {}
        attnv_T = {}

        def emit_scores(g):
            p, j = g // 16, g % 16
            pn = "P01" if g % 2 == 0 else "P23"
            w = d.pwait([(pn, 0), (pn, 1)])
            if g < 4:
                w = d.merge(w, {"V": max(qn_red[7], self.ck_V)})
            if j < 2:
                w = d.merge(w, {"V": qev[p]})
            d.t(w, lambda e, pn=pn, j=j, p=p: e.matmul(
                d.P[pn][:, 0:512],
                AR[0:64, CKV2 + 128 * j:CKV2 + 128 * j + 128],
                AR[0:64, QTB + 512 * p:QTB + 512 * p + 512],
                start=True, stop=True, tile_position=(0, 0)), inc=False)
            smT = d.t({}, lambda e, pn=pn, j=j, p=p: e.matmul(
                d.P[pn][:, 512:1024],
                AR[64:128, CKV2 + 128 * j:CKV2 + 128 * j + 128],
                AR[64:128, QTB + 512 * p:QTB + 512 * p + 512],
                start=True, stop=True, tile_position=(64, 0)), inc=True)
            we = {"T": max(smT * 0, smT), "G3": gcb}
            we["T"] = max(smT, attnv_T.get(g - 2, st.get("attnv_7", 0)))
            ss = d.s(we, lambda e, pn=pn, g=g, cbslot=cbslot: e.activation(
                AR[:, ETB + 1024 * (g % 2):ETB + 1024 * (g % 2) + 1024],
                d.P[pn][:, 0:1024], AF.Exp,
                bias=d.Cb[:, cbslot:cbslot + 1], scale=0.125))
            exp_S[g] = ss
            d.pread([(pn, 0), (pn, 1)], "S", ss)

        def emit_attnv(g):
            p, ja = g // 16, g % 16
            pb = "P45" if p % 2 == 0 else "P67"
            wa = {"S": exp_S[g]}
            if ja == 0:
                wa = d.merge(wa, {"V": vg_done[p]},
                             d.pwait([(pb, 0), (pb, 1)]))
            vaoff = VA + 2080 * (p % 2)
            d.t(wa, lambda e, ja=ja, vaoff=vaoff, pb=pb, g=g: e.matmul(
                d.P[pb][0:65, 0:512],
                AR[:, vaoff + 130 * ja:vaoff + 130 * ja + 65],
                AR[:, ETB + 1024 * (g % 2):ETB + 1024 * (g % 2) + 512],
                start=(ja == 0), stop=(ja == 15)), inc=False)
            at = d.t({}, lambda e, ja=ja, vaoff=vaoff, pb=pb, g=g: e.matmul(
                d.P[pb][0:65, 512:1024],
                AR[:, vaoff + 130 * ja + 65:vaoff + 130 * ja + 130],
                AR[:, ETB + 1024 * (g % 2) + 512:
                   ETB + 1024 * (g % 2) + 1024],
                start=(ja == 0), stop=(ja == 15)), inc=True)
            attnv_T[g] = at
            if ja == 15:
                emit_norm(p)

        nst = {}

        def norm_finish(pp):
            bcb = d.gsc if pp % 2 == 0 else d.scr
            self._rv_buf[pp % 2] = d.v(
                {"G0": nst[pp]}, lambda e, bcb=bcb: e.reciprocal(
                    d.xs[0:64, 0:1024], bcb[0:64, 0:1024]))
            d.v({}, lambda e, pp=pp: e.tensor_tensor(
                AR[0:64, YT1 + 512 * pp:YT1 + 512 * pp + 512],
                d.outst[0:64, 0:512], d.xs[0:64, 0:512], ALU.mult))
            y2 = d.v({}, lambda e, pp=pp: e.tensor_tensor(
                AR[64:128, YT1 + 512 * pp:YT1 + 512 * pp + 512],
                d.outst[0:64, 512:1024], d.xs[0:64, 512:1024],
                ALU.mult))
            self._ylast = y2

        def emit_norm(p):
            pb = "P45" if p % 2 == 0 else "P67"
            if p >= 1:
                norm_finish(p - 1)
            d.v({"T": attnv_T[16 * p + 15], "G0": self._dg1_prev},
                lambda e, pb=pb: e.tensor_copy(
                d.outst[0:65, 0:512], d.P[pb][0:65, 0:512]))
            cpv = d.v({}, lambda e, pb=pb: e.tensor_copy(
                d.outst[0:65, 512:1024], d.P[pb][0:65, 512:1024]))
            d.pread([(pb, 0), (pb, 1)], "V", cpv)
            dg1 = d.dg({"V": cpv, "G0": self._bcg_prev},
                       lambda e: e.dma_start(
                out=d.R3[0:1, 0:1024], in_=d.outst[64:65, 0:1024]))
            self._dg1_prev = dg1
            bcb = d.gsc if p % 2 == 0 else d.scr
            bcg = d.dg({"G0": dg1, "V": self._rv_buf.get(p % 2, 0)},
                       lambda e, bcb=bcb: e.dma_start(
                out=bcb[0:64, 0:1024],
                in_=bass.AP(d.R3, 0, [[1, 1], [0, 64], [1, 1024]])))
            self._bcg_prev = bcg
            nst[p] = bcg

        for g in range(130):
            if conv_left and g % 10 == 5:
                self.conv_piece(conv_left.pop(0))
            if g >= 2:
                emit_attnv(g - 2)
            if g < 128:
                emit_scores(g)
            if g % 16 in (8, 10, 12, 14) and g // 16 < 7:
                p = g // 16
                part = (g % 16 - 8) // 2
                vg_done[p + 1] = vgen_part(
                    p + 1, 4 * part, 4 * part + 4,
                    {"T": attnv_T.get(16 * (p - 1) + 15,
                                      st.get("attnv_7", 0))})
        assert not conv_left
        self._pend_proj = (b, blk, norm_finish)
        st["attnv_6"] = attnv_T[16 * 6 + 15]
        st["attnv_7"] = attnv_T[16 * 7 + 15]
        st["exp_m2"] = st.get("exp_m1", self.entS)
        st["exp_m1"] = exp_S[127]
        st["gcb_m2"] = st.get("gcb_m1", 0)
        st["gcb_m1"] = gcb

    def _qnslot(self, k):
        return [(self.gsc, 0), (self.gsc, 512),
                (self.xs, 1024), (self.xs, 1536)][k % 4]

    def _emit_proj(self, b, blk, ylast):
        d = self
        AR = d.AR
        tq0 = 512 * blk
        slots = [("P01", 0), ("P01", 512), ("P23", 0), ("P23", 512)]
        self._ylast_final = ylast
        for cc in range(8):
            pn, off = slots[cc % 4]
            pt = d.P[pn]
            for k in range(8):
                w = {}
                if k == 0:
                    w = d.pwait([(pn, off // 512)])
                    if cc == 0:
                        w = d.merge(w, {"V": ylast})
                d.t(w, lambda e, pt=pt, off=off, k=k, cc=cc: e.matmul(
                    pt[:, off:off + 512],
                    AR[:, WPC + 1024 * k + 128 * cc:
                       WPC + 1024 * k + 128 * cc + 128],
                    AR[:, YT1 + 512 * k:YT1 + 512 * k + 512],
                    start=(k == 0), stop=(k == 7)), inc=(k == 7))
            tt = d.cnt["T"]
            ev = d.v({"T": tt}, lambda e, pt=pt, off=off, cc=cc, tq0=tq0:
                     e.tensor_scalar(
                d.sT[:, 2048 * cc + tq0:2048 * cc + tq0 + 512],
                pt[:, off:off + 512], d.bpc[:, cc:cc + 1], None, ALU.add))
            d.pread([(pn, off // 512)], "V", ev)

    def ffn(self, b):
        d = self
        if self._pend_proj is not None:
            pb_, pblk_, fin_ = self._pend_proj
            self._pend_proj = None
            fin_(7)
            self._emit_proj(pb_, pblk_, self._ylast)
        AR = d.AR
        xrow = 1024 * b
        ffnT, ffnS = self.cnt["T"], self.cnt["S"]
        hp_v = {}
        silu_S = {}
        cv_v = {}
        cvg_v = {}
        gu_byi = {}
        prev_half_T = 0
        for half in range(2):
            for hc in range(24):
                i = half * 24 + hc
                par = i % 2
                ga_pn = "P01" if par == 0 else "P45"
                gb_pn = "P23" if par == 0 else "P67"
                if i == 0:
                    wq_ = {"V": max(self.entV, self.cvp_c.get(47, 0))}
                elif hc == 0:
                    wq_ = {"V": cvg_v[i - 1] if False else cv_v[i - 1]}
                else:
                    wq_ = {"S": cvg_v[i - 1]}
                wq2 = dict(wq_) if i == 0 or hc == 0 else {"S": cv_v[i - 1]}
                qg = d.dq(wq_, lambda e, hc=hc: e.dma_start(
                    out=bass.AP(d.IT, WFI,
                                [[2048, 128], [128, 8], [1, 128]]),
                    in_=bass.AP(d.d_wg.tensor, 128 * hc,
                                [[3072, 128], [393216, 8], [1, 128]])),
                          q="Q5")
                qu = d.dq(wq2, lambda e, hc=hc: e.dma_start(
                    out=bass.AP(d.IT, WFI + 1024,
                                [[2048, 128], [128, 8], [1, 128]]),
                    in_=bass.AP(d.d_wu.tensor, 128 * hc,
                                [[3072, 128], [393216, 8], [1, 128]])),
                          q="Q7")
                wcv = {"Q5": qg, "T": gu_byi.get(i - 2, 0)}
                if i == 0:
                    wcv = d.merge(wcv, {"T": ffnT, "S": ffnS})
                cvg_v[i] = d.s(wcv, lambda e, par=par: e.copy(
                    AR[:, WFB + 2048 * par:WFB + 2048 * par + 1024],
                    d.IT[:, WFI:WFI + 1024]))
                cv_v[i] = d.s({"Q7": qu}, lambda e, par=par: e.copy(
                    AR[:, WFB + 2048 * par + 1024:WFB + 2048 * par + 2048],
                    d.IT[:, WFI + 1024:WFI + 2048]))
                guT = {}
                for m in range(2):
                    pn = ga_pn if m == 0 else gb_pn
                    boff = 1024 * m
                    for tb in range(2):
                        w = {}
                        if m == 0 and tb == 0:
                            w = d.merge({"S": cvg_v[i]},
                                        d.pwait([(ga_pn, 0), (ga_pn, 1),
                                                 (gb_pn, 0), (gb_pn, 1)]))
                        if m == 1 and tb == 0:
                            w = {"S": cv_v[i]}
                        for c in range(8):
                            d.t(w if c == 0 else {}, lambda e, pn=pn, tb=tb,
                                c=c, par=par, boff=boff, half=half: e.matmul(
                                d.P[pn][:, 512 * tb:512 * tb + 512],
                                AR[:, WFB + 2048 * par + boff + 128 * c:
                                   WFB + 2048 * par + boff + 128 * c + 128],
                                d.xn[:, 2048 * c + 1024 * half + 512 * tb:
                                     2048 * c + 1024 * half + 512 * tb + 512],
                                start=(c == 0), stop=(c == 7)),
                                inc=(c == 7 and tb == 1))
                    guT[m] = d.cnt["T"]
                gu_byi[i] = guT[1]
                wsl = {"T": guT[0]}
                if i >= 2:
                    wsl["V"] = hp_v[i - 2]
                ssl = d.s(wsl, lambda e, ga_pn=ga_pn, par=par: e.activation(
                    AR[:, APR + 1024 * par:APR + 1024 * par + 1024],
                    d.P[ga_pn][:, 0:1024], AF.Silu, scale=S127))
                silu_S[i] = ssl
                d.pread([(ga_pn, 0), (ga_pn, 1)], "S", ssl)
                hv = d.v({"S": ssl, "T": guT[1]},
                         lambda e, gb_pn=gb_pn, par=par, hc=hc: e.tensor_tensor(
                    AR[:, HP + 1024 * hc:HP + 1024 * hc + 1024],
                    AR[:, APR + 1024 * par:APR + 1024 * par + 1024],
                    d.P[gb_pn][:, 0:1024], ALU.mult))
                hp_v[i] = hv
                d.pread([(gb_pn, 0), (gb_pn, 1)], "V", hv)
            dslots = [("P01", 0), ("P01", 512), ("P23", 0), ("P23", 512)]
            o2_v = {}
            dma_g = {}
            down_byc = {}
            wd_q = {}
            hlast = hp_v[half * 24 + 23]
            slast = silu_S[half * 24 + 23]
            for cc in range(8):
                par = cc % 2
                wl = {"T": max(down_byc.get(cc - 2, 0),
                               prev_half_T if cc < 2 else 0)}
                if cc < 2:
                    wl = d.merge(wl, {"V": hlast, "S": slast},
                                 self.wdb_done if half == 0 else {})
                wd_q[cc] = d.dq(wl, lambda e, cc=cc, par=par: e.dma_start(
                    out=bass.AP(d.AR, WDB + 3072 * par,
                                [[ACOLS, 128], [128, 24], [1, 128]]),
                    in_=bass.AP(d.d_wdb.tensor, 128 * cc,
                                [[1024, 128], [131072, 24], [1, 128]])),
                               q="Q6" if par == 0 else "Q4")
                self._wd_cv_last = 0
                for tb in range(2):
                    s4 = (2 * cc + tb) % 4
                    pn, off = dslots[s4]
                    pt = d.P[pn]
                    w = d.merge({"Q6" if par == 0 else "Q4": wd_q[cc]},
                                d.pwait([(pn, off // 512)]))
                    for hcc in range(24):
                        d.t(w if hcc == 0 else {}, lambda e, pt=pt, off=off,
                            hcc=hcc, par=par, tb=tb: e.matmul(
                            pt[:, off:off + 512],
                            AR[:, WDB + 3072 * par + 128 * hcc:
                               WDB + 3072 * par + 128 * hcc + 128],
                            AR[:, HP + 1024 * hcc + 512 * tb:
                               HP + 1024 * hcc + 512 * tb + 512],
                            start=(hcc == 0), stop=(hcc == 23)),
                            inc=(hcc == 23))
                    tt = d.cnt["T"]
                    down_byc[cc] = tt
                    s2 = (2 * cc + tb) % 2
                    wgs = {"T": tt, "V": o2_v.get(2 * cc + tb - 2, 0)}
                    gss = d.s(wgs, lambda e, pt=pt, off=off, s2=s2:
                              e.activation(d.gsc[:, 512 * s2:512 * s2 + 512],
                                           pt[:, off:off + 512], AF.Silu,
                                           scale=S127SQ))
                    d.pread([(pn, off // 512)], "S", gss)
                    xslot = (2 * cc + tb) % 4
                    qn_ = "Q%d" % xslot
                    lq = d.dq({"V": max(o2_v.get(2 * cc + tb - 4, 0),
                                        self._ylast_final
                                        if 2 * cc + tb < 4 else 0)},
                              lambda e, cc=cc, half=half, tb=tb, xslot=xslot:
                              e.dma_start(
                        out=d.xs[:, 512 * xslot:512 * xslot + 512],
                        in_=d.d_x[xrow + 128 * cc:xrow + 128 * cc + 128,
                                  1024 * half + 512 * tb:
                                  1024 * half + 512 * tb + 512]), q=qn_)
                    d.v({"S": gss}, lambda e, s2=s2, cc=cc, half=half, tb=tb:
                        e.tensor_tensor(
                        d.outst[:, 0:512],
                        d.gsc[:, 512 * s2:512 * s2 + 512],
                        d.sT[:, 2048 * cc + 1024 * half + 512 * tb:
                             2048 * cc + 1024 * half + 512 * tb + 512],
                        ALU.mult))
                    gq = "G%d" % (1 + s2)
                    ov = d.v({qn_: lq, gq: dma_g.get(2 * cc + tb - 2, 0)},
                             lambda e, s2=s2, xslot=xslot: e.tensor_tensor(
                        d.outst[:, 512 + 512 * s2:512 + 512 * s2 + 512],
                        d.outst[:, 0:512],
                        d.xs[:, 512 * xslot:512 * xslot + 512], ALU.add))
                    o2_v[2 * cc + tb] = ov
                    dma_g[2 * cc + tb] = d.dg(
                        {"V": ov}, lambda e, cc=cc, half=half, tb=tb, s2=s2:
                        e.dma_start(
                            out=d.d_out[xrow + 128 * cc:xrow + 128 * cc + 128,
                                        1024 * half + 512 * tb:
                                        1024 * half + 512 * tb + 512],
                            in_=d.outst[:, 512 + 512 * s2:
                                        512 + 512 * s2 + 512]), q=gq)
            prev_half_T = d.cnt["T"]


def kernel(**inputs):
    f32 = np.float32
    x = np.asarray(inputs["L0_attn_x"], f32)
    wq = np.ascontiguousarray(np.asarray(inputs["L0_attn_wq_d"], f32))
    bq = np.ascontiguousarray(np.asarray(inputs["L0_attn_bq_d"], f32).reshape(512, 1))
    wqk = np.asarray(inputs["L0_attn_w_qk"], f32)
    bqk = np.ascontiguousarray(np.asarray(inputs["L0_attn_b_qk"], f32).reshape(1024, 1))
    wkv = np.asarray(inputs["L0_attn_wkv_d"], f32)
    bkv = np.ascontiguousarray(np.asarray(inputs["L0_attn_bkv_d"], f32).reshape(64, 1))
    wvu = np.asarray(inputs["L0_attn_wv_u"], f32)
    bvu = np.asarray(inputs["L0_attn_bv_u"], f32)
    wp = np.asarray(inputs["L0_attn_w_proj"], f32)
    bp = np.ascontiguousarray(np.asarray(inputs["L0_attn_b_proj"], f32).reshape(1024, 1))
    wg = np.asarray(inputs["L0_ffn_w_gate"], np.int32)
    wu = np.asarray(inputs["L0_ffn_w_up"], np.int32)
    wd = np.asarray(inputs["L0_ffn_w_down"], np.int32)

    wqkT = np.ascontiguousarray(wqk.T)
    wkvT = np.ascontiguousarray(wkv.T)
    wpT = np.ascontiguousarray(wp.T)
    wgT = np.ascontiguousarray(wg.T)
    wuT = np.ascontiguousarray(wu.T)
    wdT = np.ascontiguousarray(wd.T)
    wvuT = wvu.T
    wvu_aug = np.zeros((65, 1040), f32)
    for h in range(16):
        wvu_aug[0:64, 65 * h:65 * h + 64] = wvuT[:, 64 * h:64 * h + 64]
        wvu_aug[64, 65 * h:65 * h + 64] = bvu[64 * h:64 * h + 64]
        wvu_aug[64, 65 * h + 64] = 1.0
    wvu_aug = np.ascontiguousarray(wvu_aug)

    nc = KB().build()
    in_maps = []
    for c in range(8):
        xsh = np.ascontiguousarray(
            x[2 * c:2 * c + 2].transpose(0, 2, 1)).reshape(2048, 2048)
        in_maps.append({
            "xT": xsh, "wq": wq, "wqkT": wqkT, "bq": bq, "bqk": bqk,
            "wkvT": wkvT, "bkv": bkv, "wvu": wvu_aug, "wpT": wpT, "bp": bp,
            "wgT": wgT, "wuT": wuT, "wdT": wdT,
        })
    res = run_bass_kernel_spmd(nc, in_maps, core_ids=list(range(8)))
    out = np.empty((16, 2048, 1024), f32)
    for c in range(8):
        o = np.asarray(res.results[c]["outT"]).reshape(2, 1024, 2048)
        out[2 * c:2 * c + 2] = o.transpose(0, 2, 1)
    return out


if __name__ == "__main__":
    KB().build()
    print("build ok")
